# version 14
# baseline (speedup 1.0000x reference)
import numpy as np
import ml_dtypes
from contextlib import ExitStack

import concourse.bass as bass
import concourse.mybir as mybir
from concourse.bass_utils import run_bass_kernel_spmd

F32 = mybir.dt.float32
BF16 = mybir.dt.bfloat16
AF = mybir.ActivationFunctionType
ALU = mybir.AluOpType
AX = mybir.AxisListType

S = 4096
D = 1024
NT = S // 128
NCH = 8
KT = 32
NG = 4
NP = 8
HID = 2048
NE = 10
NQ = 4
QT = S // NQ
EPS = 1e-5

ENGINES = ("sync", "gpsimd", "tensor", "vector", "scalar")


class Chan:
    __slots__ = ("sem", "name", "n")

    def __init__(self, sem, name):
        self.sem = sem
        self.name = name
        self.n = 0


AUTOCHAIN = ("vector", "scalar")


class Prog:
    def __init__(self, nc, es):
        self.nc = nc
        self.es = es
        self.ops = {e: [] for e in ENGINES}
        self.cseq = {}
        self.last_marker = {}

    def init_autochain(self):
        for e in AUTOCHAIN:
            self.cseq[e] = self.chan(f"cseq_{e}")

    def chan(self, name):
        sem = self.es.enter_context(self.nc.semaphore(name))
        return Chan(sem, name)

    def scope(self, name):
        self.cur_scope = name

    def emit(self, eng, fn, waits=(), inc=None, dma=False, nochain=False):
        w = [(c, v) for (c, v) in waits if v > 0]
        if eng in self.cseq and not dma:
            lm = self.last_marker.get(eng)
            if lm is not None and not nochain:
                w.append(lm)
            if inc is None:
                inc = self.cseq[eng]
        self.ops[eng].append((w, fn, inc, dma, getattr(self, "cur_scope", "x")))
        if inc is not None:
            inc.n += 16 if dma else 1
            if eng in self.cseq and not dma:
                self.last_marker[eng] = (inc, inc.n)
        return inc.n if inc is not None else None

    def build(self, block):
        def mk(name):
            def body(eng):
                hwm = {}
                for w, fn, inc, dma in self.ops[name]:
                    for c, v in w:
                        if hwm.get(c.name, 0) < v:
                            eng.wait_ge(c.sem, v)
                            hwm[c.name] = v
                ins = fn(eng)
                if inc is not None:
                    ins.then_inc(inc.sem, 16 if dma else 1)
            return body

        raise RuntimeError("use build2")

    def build2(self, block):
        prog = self
        nc = self.nc

        def run_stream(name, eng):
            hwm = {}
            for w, fn, inc, dma, sc in prog.ops[name]:
                with nc.named_scope(sc):
                    for c, v in w:
                        if hwm.get(c.name, 0) < v:
                            eng.wait_ge(c.sem, v)
                            hwm[c.name] = v
                    ins = fn(eng)
                    if inc is not None:
                        ins.then_inc(inc.sem, 16 if dma else 1)

        @block.sync
        def _(e):
            run_stream("sync", e)

        @block.gpsimd
        def _(e):
            run_stream("gpsimd", e)

        @block.tensor
        def _(e):
            run_stream("tensor", e)

        @block.vector
        def _(e):
            run_stream("vector", e)

        @block.scalar
        def _(e):
            run_stream("scalar", e)

    def simulate(self):
        counts = {}
        idx = {e: 0 for e in ENGINES}
        total = sum(len(v) for v in self.ops.values())
        done = 0
        while done < total:
            progress = False
            for e in ENGINES:
                while idx[e] < len(self.ops[e]):
                    w, fn, inc, dma, _sc = self.ops[e][idx[e]]
                    if all(counts.get(c.name, 0) >= v for c, v in w):
                        if inc is not None:
                            counts[inc.name] = counts.get(inc.name, 0) + (16 if dma else 1)
                        idx[e] += 1
                        done += 1
                        progress = True
                    else:
                        break
            if not progress:
                blocked = {}
                for e in ENGINES:
                    if idx[e] < len(self.ops[e]):
                        w, _, _, _, _sc = self.ops[e][idx[e]]
                        blocked[e] = [(c.name, v, counts.get(c.name, 0)) for c, v in w
                                      if counts.get(c.name, 0) < v]
                raise RuntimeError(f"deadlock at op idx {idx}; blocked: {blocked}")


def build(dbg=False):
    nc = bass.Bass("TRN2", target_bir_lowering=False)

    x_d = nc.dram_tensor("x", [S, D], F32, kind="ExternalInput").ap()
    wq_d = nc.dram_tensor("wq", [128, 8, D], F32, kind="ExternalInput").ap()
    wk_d = nc.dram_tensor("wk", [128, 8, D], F32, kind="ExternalInput").ap()
    wv_d = nc.dram_tensor("wv", [128, 8, D], F32, kind="ExternalInput").ap()
    rwt_d = nc.dram_tensor("rwt", [128, 8, 8], F32, kind="ExternalInput").ap()
    eb_d = nc.dram_tensor("eb", [8, 1], F32, kind="ExternalInput").ap()
    wa1t_d = nc.dram_tensor("wa1t", [NE, 4, 128, 8, 512], F32, kind="ExternalInput").ap()
    wa3t_d = nc.dram_tensor("wa3t", [NE, 4, 128, 8, 512], F32, kind="ExternalInput").ap()
    wa2t_d = nc.dram_tensor("wa2t", [NE, 8, 128, 16, 128], F32, kind="ExternalInput").ap()
    id32_d = nc.dram_tensor("id32", [128, 128], F32, kind="ExternalInput").ap()
    onesbf_d = nc.dram_tensor("onesbf", [128, 1], BF16, kind="ExternalInput").ap()
    ones32_d = nc.dram_tensor("ones32", [1, 128], F32, kind="ExternalInput").ap()
    out_d = nc.dram_tensor("out", [S, D], F32, kind="ExternalOutput").ap()

    skind = "ExternalOutput" if dbg else "Internal"
    h_scr = nc.dram_tensor("h_scr", [S, D], BF16, kind=skind).ap()
    qT_scr = nc.dram_tensor("qT_scr", [NCH, 128, NP, 512], BF16, kind=skind).ap()
    kT_scr = nc.dram_tensor("kT_scr", [NP, 128, S], BF16, kind=skind).ap()
    v_scr = nc.dram_tensor("v_scr", [NP, 128, NT, 2, 64], BF16, kind=skind).ap()
    xp_scr = nc.dram_tensor("xp_scr", [S, D], F32, kind=skind).ap()
    h2_scr = nc.dram_tensor("h2_scr", [S, D], BF16, kind=skind).ap()
    cwT_scr = nc.dram_tensor("cwT_scr", [8, S], F32, kind=skind).ap()

    es0 = ExitStack()
    with es0:
        p = Prog(nc, es0)
        p.init_autochain()
        SB = lambda n, sh, dt: es0.enter_context(nc.sbuf_tensor(n, sh, dt))

        id32 = SB("id32s", [128, 128], F32)
        onesbf = SB("onesbfs", [128, 1], BF16)
        ones32 = SB("ones32s", [1, 128], F32)
        biasg = SB("biasg", [128, 8], F32)
        ebias = SB("ebiass", [8, 1], F32)
        epsb = SB("epsb", [128, 1], F32)
        rwt_sb = SB("rwt_sb", [128, 8, 8], BF16)

        d_c = p.chan("d_c")
        d_x = p.chan("d_x")
        d_h = p.chan("d_h")
        d_hT = p.chan("d_hT")
        d_wqkv = p.chan("d_wqkv")
        d_qk = p.chan("d_qk")
        d_v = p.chan("d_v")
        d_qc = p.chan("d_qc")
        d_h2p = p.chan("d_h2p")
        d_kv = p.chan("d_kv")
        d_x2 = p.chan("d_x2")
        d_xp = p.chan("d_xp")
        d_h2 = p.chan("d_h2")
        d_h2T = p.chan("d_h2T")
        d_w13 = p.chan("d_w13")
        d_w2 = p.chan("d_w2")
        d_rw = p.chan("d_rw")
        d_cws = p.chan("d_cws")
        d_cwl = p.chan("d_cwl")
        d_xpl = p.chan("d_xpl")
        d_out = p.chan("d_out")
        c_sq = p.chan("c_sq")
        c_var = p.chan("c_var")
        c_rstd = p.chan("c_rstd")
        c_ln = p.chan("c_ln")
        c_qk = p.chan("c_qk")
        c_qkev = p.chan("c_qkev")
        c_sq2 = p.chan("c_sq2")
        c_nrm = p.chan("c_nrm")
        c_nred = p.chan("c_nred")
        c_mbv = p.chan("c_mbv")
        c_mbb = p.chan("c_mbb")
        c_mb = p.chan("c_mb")
        c_v = p.chan("c_v")
        c_vev = p.chan("c_vev")
        c_sc = p.chan("c_sc")
        c_exp = p.chan("c_exp")
        c_av = p.chan("c_av")
        c_rcp = p.chan("c_rcp")
        c_oev = p.chan("c_oev")
        c_otr = p.chan("c_otr")
        c_div = p.chan("c_div")
        c_lno = p.chan("c_lno")
        c_xp = p.chan("c_xp")
        c_res = p.chan("c_res")
        c_rv = p.chan("c_rv")
        c_rr = p.chan("c_rr")
        c_rs = p.chan("c_rs")
        c_lg = p.chan("c_lg")
        c_lgev = p.chan("c_lgev")
        c_ltr = p.chan("c_ltr")
        c_tk = p.chan("c_tk")
        c_ctr = p.chan("c_ctr")
        c_cmb = p.chan("c_cmb")
        c_cw = p.chan("c_cw")
        c_cwm = p.chan("c_cwm")
        c_z1 = p.chan("c_z1")
        c_z3 = p.chan("c_z3")
        c_si = p.chan("c_si")
        c_zm = p.chan("c_zm")
        c_y2 = p.chan("c_y2")
        c_fu = p.chan("c_fu")
        c_ftr = p.chan("c_ftr")
        c_fev = p.chan("c_fev")
        c_ftc = p.chan("c_ftc")
        c_fv = p.chan("c_fv")
        c_fr = p.chan("c_fr")
        c_fs = p.chan("c_fs")
        c_oadd = p.chan("c_oadd")

        c_cst = p.chan("c_cst")
        h2T_pre = SB("h2T_pre", [128, 8, QT], BF16)
        w1c = SB("w1c", [128, 2, 8, 512], BF16)
        w3c = SB("w3c", [128, 2, 8, 512], BF16)
        p.scope("consts")
        p.emit("vector", lambda e: e.memset(epsb[:], EPS), inc=c_cst)
        p.emit("sync", lambda e, a=id32, b=id32_d: e.dma_start(a[:], b[:]), inc=d_c, dma=True)
        p.emit("sync", lambda e, a=onesbf, b=onesbf_d: e.dma_start(a[:], b[:]), inc=d_c, dma=True)
        p.emit("sync", lambda e, a=ones32, b=ones32_d: e.dma_start(a[:], b[:]), inc=d_c, dma=True)
        p.emit("sync", lambda e, a=ebias, b=eb_d: e.dma_start(a[:], b[:]), inc=d_c, dma=True)
        p.emit("gpsimd", lambda e: e.dma_start(rwt_sb[:], rwt_d[:]), inc=d_rw, dma=True)

        def emit_ln(src, dst, sq, st, ch_sq, ch_var, ch_rstd, ch_out,
                    waits_src_act=(), waits_src_dve=(), extra_final_waits=()):
            p.emit("scalar",
                   lambda e, s=src, q=sq: e.activation(q, s, AF.Square),
                   waits=waits_src_act, inc=ch_sq)
            p.emit("vector",
                   lambda e, s=src, t=st: e.reduce_sum(t[:, 0:1], s, axis=AX.X),
                   waits=waits_src_dve)
            p.emit("vector",
                   lambda e, q=sq, t=st: e.reduce_sum(t[:, 1:2], q, axis=AX.X),
                   waits=[(ch_sq, ch_sq.n)])
            p.emit("vector",
                   lambda e, t=st: e.tensor_scalar(t[:, 2:3], t[:, 0:1], 1.0 / D, None, op0=ALU.mult))
            p.emit("vector",
                   lambda e, t=st: e.tensor_scalar(t[:, 3:4], t[:, 1:2], 1.0 / D, None, op0=ALU.mult))
            p.emit("vector",
                   lambda e, t=st: e.tensor_tensor(t[:, 4:5], t[:, 2:3], t[:, 2:3], op=ALU.mult))
            p.emit("vector",
                   lambda e, t=st: e.tensor_tensor(t[:, 5:6], t[:, 3:4], t[:, 4:5], op=ALU.subtract),
                   inc=ch_var)
            p.emit("scalar",
                   lambda e, t=st: e.activation(t[:, 6:7], t[:, 5:6], AF.Ln,
                                                bias=epsb[0:t.shape[0], 0:1]),
                   waits=[(ch_var, ch_var.n), (c_cst, 1)])
            p.emit("scalar",
                   lambda e, t=st: e.activation(t[:, 7:8], t[:, 6:7], AF.Exp, scale=-0.5),
                   inc=ch_rstd)
            p.emit("vector",
                   lambda e, s=src, d=dst, t=st: e.tensor_scalar(
                       d, s, t[:, 2:3], t[:, 7:8], op0=ALU.subtract, op1=ALU.mult),
                   waits=[(ch_rstd, ch_rstd.n)] + list(extra_final_waits), inc=ch_out)

        es1 = ExitStack()
        S1 = lambda n, sh, dt: es1.enter_context(nc.sbuf_tensor(n, sh, dt))
        P1 = lambda n, sh, dt: es1.enter_context(nc.psum_tensor(n, sh, dt))
        with es1:
            xb = S1("xb", [128, 2, D], F32)
            sqb = S1("sqb", [128, 2, D], F32)
            stb = S1("stb", [128, 2, 8], F32)
            hb = S1("hb", [128, 2, D], BF16)
            hT = S1("hT", [128, 8, S], BF16)
            wq_sb = S1("wq_sb", [128, 8, D], BF16)
            wk_sb = S1("wk_sb", [128, 8, D], BF16)
            wv_sb = S1("wv_sb", [128, 8, D], BF16)
            ev_qk = S1("ev_qk", [128, 2, 512], BF16)
            sq2 = S1("sq2", [128, 2, 512], BF16)
            nmax = S1("nmax", [1, 256], F32)
            mbh = S1("mbh", [1, 64], F32)
            ev_v = S1("ev_v", [128, 2, 512], BF16)
            pq = P1("pq", [128, 4, 512], F32)
            pnrm = P1("pnrm", [128, 2, 512], F32)
            pbc = P1("pbc", [128, 512], F32)

            for w_sb, w_d in ((wq_sb, wq_d), (wk_sb, wk_d), (wv_sb, wv_d)):
                p.emit("gpsimd",
                       lambda e, a=w_sb, b=w_d: e.dma_start(a[:], b[:]),
                       inc=d_wqkv, dma=True)

            qk_store_snap = [0, 0]
            v_store_snap = [0, 0]
            nrm_snap_par = [0, 0]
            nred_snap = [0]
            pq_free = [None, None, None, None]
            ln_done = [0] * NT
            sq_done = [0] * NT
            qkseq = [0]
            p.scope("lnqkv")

            def emit_ln_tile(tt):
                par = tt % 2
                wl = []
                if tt >= 2:
                    wl = [(c_ln, ln_done[tt - 2]), (c_sq, sq_done[tt - 2])]
                p.emit("sync",
                       lambda e, a=xb, pr=par, t=tt: e.dma_start(
                           a[:, pr, :], x_d[t * 128:(t + 1) * 128, :]),
                       waits=wl, inc=d_x, dma=True)
                hwait = [(d_h, 16 * (tt - 1))] if tt >= 2 else []
                emit_ln(xb[:, par, :], hb[:, par, :], sqb[:, par, :], stb[:, par, :],
                        c_sq, c_var, c_rstd, c_ln,
                        waits_src_act=[(d_x, d_x.n)],
                        waits_src_dve=[(d_x, d_x.n)],
                        extra_final_waits=hwait)
                sq_done[tt] = c_sq.n
                ln_done[tt] = c_ln.n
                p.emit("sync",
                       lambda e, a=hb, pr=par, t=tt: e.dma_start(
                           h_scr[t * 128:(t + 1) * 128, :], a[:, pr, :]),
                       waits=[(c_ln, c_ln.n)], inc=d_h, dma=True)

            def emit_hT_chunk(cc2):
                for dt in range(8):
                    p.emit("scalar",
                           lambda e, d=dt, cc=cc2: e.dma_start_transpose(
                               hT[:, d, cc * 512:(cc + 1) * 512],
                               h_scr[cc * 512:(cc + 1) * 512, d * 128:(d + 1) * 128]),
                           waits=[(d_h, 16 * (4 * cc2 + 4))], inc=d_hT, dma=True)

            for tt in range(4):
                emit_ln_tile(tt)
            emit_hT_chunk(0)
            pending_nrm = []
            for ch in range(NCH):
                lnq = []
                if ch + 1 < NCH:
                    for tt in range(4 * ch + 4, 4 * ch + 8):
                        lnq.append(lambda t=tt: emit_ln_tile(t))
                    lnq.append(lambda c2=ch + 1: emit_hT_chunk(c2))

                p.scope(f"qk{ch}")
                for pr_i in range(NP):
                    for proj in range(2):
                        w_sb = (wq_sb, wk_sb)[proj]
                        par = qkseq[0] % 2
                        p4 = qkseq[0] % 4
                        qkseq[0] += 1

                        def qk_mms(e, w=w_sb, pi=pr_i, c=ch, pr=p4):
                            ins = None
                            for kt in range(8):
                                ins = e.matmul(pq[:, pr, :], w[:, kt, pi * 128:(pi + 1) * 128],
                                               hT[:, kt, c * 512:(c + 1) * 512],
                                               start=(kt == 0), stop=(kt == 7))
                            return ins
                        qw = [(d_wqkv, 16 * (proj + 1)), (d_hT, 16 * 8 * (ch + 1))]
                        if pq_free[p4] is not None:
                            qw.append(pq_free[p4])
                        p.emit("tensor", qk_mms, waits=qw, inc=c_qk)
                        if len(pending_nrm) >= 2:
                            pending_nrm.pop(0)()
                        if lnq and qkseq[0] % 3 == 2:
                            lnq.pop(0)()
                        p.emit("vector",
                               lambda e, pr=par, pp=p4: e.tensor_copy(
                                   ev_qk[:, pr, :], pq[:, pp, :]),
                               waits=[(c_qk, c_qk.n), (d_qk, qk_store_snap[par])],
                               inc=c_qkev)
                        pq_free[p4] = (c_qkev, c_qkev.n)
                        p.emit("vector",
                               lambda e, pr=par: e.tensor_tensor(
                                   sq2[:, pr, :], ev_qk[:, pr, :], ev_qk[:, pr, :], op=ALU.mult),
                               waits=[(c_nrm, nrm_snap_par[par])],
                               inc=c_sq2)

                        def nrm_mms(e, pr=par):
                            e.matmul(pnrm[0:1, 0, :], onesbf[0:64, 0:1], sq2[0:64, pr, :],
                                     start=True, stop=True, tile_position=(0, 0),
                                     skip_group_check=True)
                            return e.matmul(pnrm[0:1, 1, :], onesbf[64:128, 0:1],
                                            sq2[64:128, pr, :],
                                            start=True, stop=True, tile_position=(64, 0),
                                            skip_group_check=True)

                        def emit_nrm(par=par, sq2_cnt=c_sq2.n, nrm_fn=nrm_mms,
                                     idx2=((pr_i * 2 + proj) * 8 + ch) * 2):
                            p.emit("tensor", nrm_fn,
                                   waits=[(c_sq2, sq2_cnt), (c_nred, nred_snap[0]),
                                          (d_c, 32)],
                                   inc=c_nrm)
                            nrm_snap_par[par] = c_nrm.n
                            p.emit("vector",
                                   lambda e, i2=idx2: e.reduce_max(
                                       nmax[0:1, i2:i2 + 2], pnrm[0:1, 0:2, :],
                                       axis=AX.X),
                                   waits=[(c_nrm, c_nrm.n)], inc=c_nred)
                            nred_snap[0] = c_nred.n
                        pending_nrm.append(emit_nrm)
                        if proj == 0:
                            p.emit("sync",
                                   lambda e, pi=pr_i, c=ch, pr=par: e.dma_start(
                                       qT_scr[c, :, pi, :], ev_qk[:, pr, :]),
                                   waits=[(c_sq2, c_sq2.n)], inc=d_qk, dma=True)
                        else:
                            p.emit("sync",
                                   lambda e, pi=pr_i, c=ch, pr=par: e.dma_start(
                                       kT_scr[pi, :, c * 512:(c + 1) * 512], ev_qk[:, pr, :]),
                                   waits=[(c_sq2, c_sq2.n)], inc=d_qk, dma=True)
                        qk_store_snap[par] = d_qk.n

                p.scope(f"v{ch}")
                while pending_nrm:
                    pending_nrm.pop(0)()
                while lnq:
                    lnq.pop(0)()
                for tt in range(4 * ch, 4 * ch + 4):
                    for half in range(2):
                        par = qkseq[0] % 2
                        p4 = qkseq[0] % 4
                        qkseq[0] += 1

                        def v_mms(e, t=tt, h=half, pr=p4):
                            ins = None
                            for kt in range(8):
                                ins = e.matmul(pq[:, pr, :], hT[:, kt, t * 128:(t + 1) * 128],
                                               wv_sb[:, kt, h * 512:(h + 1) * 512],
                                               start=(kt == 0), stop=(kt == 7))
                            return ins
                        vw = [(d_wqkv, 48), (d_hT, 16 * 8 * (ch + 1))]
                        if pq_free[p4] is not None:
                            vw.append(pq_free[p4])
                        p.emit("tensor", v_mms, waits=vw, inc=c_v)
                        p.emit("vector",
                               lambda e, pr=par, pp=p4: e.tensor_copy(
                                   ev_v[:, pr, :], pq[:, pp, :]),
                               waits=[(c_v, c_v.n), (d_v, v_store_snap[par])], inc=c_vev)
                        pq_free[p4] = (c_vev, c_vev.n)
                        p.emit("sync",
                               lambda e, t=tt, h=half, pr=par: e.dma_start(
                                   v_scr[h * 4:(h + 1) * 4, :, t, :, :].rearrange(
                                       "b p h d -> p b h d"),
                                   ev_v[:, pr, :].rearrange("p (b h d) -> p b h d",
                                                            b=4, h=2)),
                               waits=[(c_vev, c_vev.n)], inc=d_v, dma=True)
                        v_store_snap[par] = d_v.n

            p.scope("mbound")
            for pi in range(NP):
                for hl in range(2):
                    ms = pi * 8 + hl * 4
                    bq = pi * 32 + hl
                    bk = pi * 32 + 16 + hl
                    p.emit("vector",
                           lambda e, m=ms, b=bq: e.reduce_max(
                               mbh[0:1, m:m + 1], nmax[0:1, b:b + 15:2], axis=AX.X),
                           waits=[(c_nred, c_nred.n)])
                    p.emit("vector",
                           lambda e, m=ms, b=bk: e.reduce_max(
                               mbh[0:1, m + 1:m + 2], nmax[0:1, b:b + 15:2], axis=AX.X))
                    p.emit("vector",
                           lambda e, m=ms: e.tensor_tensor(
                               mbh[0:1, m + 2:m + 3], mbh[0:1, m:m + 1],
                               mbh[0:1, m + 1:m + 2], op=ALU.mult),
                           inc=c_mbv)
                    p.emit("scalar",
                           lambda e, m=ms: e.activation(
                               mbh[0:1, m + 2:m + 3], mbh[0:1, m + 2:m + 3], AF.Ln),
                           waits=[(c_mbv, c_mbv.n)])
                    p.emit("scalar",
                           lambda e, m=ms: e.activation(
                               mbh[0:1, m + 2:m + 3], mbh[0:1, m + 2:m + 3], AF.Exp, scale=0.5),
                           inc=c_rstd)
                    p.emit("vector",
                           lambda e, m=ms: e.tensor_scalar(
                               mbh[0:1, m + 3:m + 4], mbh[0:1, m + 2:m + 3],
                               -0.125, -1.0, op0=ALU.mult, op1=ALU.add),
                           waits=[(c_rstd, c_rstd.n)], inc=c_mbv)
                p.emit("vector",
                       lambda e, m=pi * 8: e.tensor_reduce(
                           mbh[0:1, m + 3:m + 4], mbh[0:1, m + 3:m + 8:4], axis=AX.X,
                           op=ALU.min),
                       inc=c_mbv)
                p.emit("tensor",
                       lambda e, m=pi * 8: e.matmul(pbc[:, 0:1], ones32[0:1, :],
                                                    mbh[0:1, m + 3:m + 4],
                                                    start=True, stop=True),
                       waits=[(c_mbv, c_mbv.n), (d_c, 48), (c_mb, c_mb.n)],
                       inc=c_mbb)
                p.emit("vector",
                       lambda e, g=pi: e.tensor_copy(biasg[:, g:g + 1], pbc[:, 0:1]),
                       waits=[(c_mbb, c_mbb.n)], inc=c_mb)

            qkv_reads_done = [(c_qk, c_qk.n), (c_v, c_v.n), (c_nrm, c_nrm.n),
                              (c_sq2, c_sq2.n), (c_qkev, c_qkev.n), (c_vev, c_vev.n),
                              (c_nred, c_nred.n), (c_mbv, c_mbv.n), (c_mb, c_mb.n),
                              (d_qk, d_qk.n), (d_v, d_v.n), (d_h, d_h.n),
                              (d_hT, d_hT.n), (d_x, d_x.n)]

        es2 = ExitStack()
        S2 = lambda n, sh, dt: es2.enter_context(nc.sbuf_tensor(n, sh, dt))
        P2 = lambda n, sh, dt: es2.enter_context(nc.psum_tensor(n, sh, dt))
        with es2:
            qTc = S2("qTc", [128, 2, 8, 512], BF16)
            kTp = S2("kTp", [128, 2, S], BF16)
            VSL = NT * 130 + 64
            vp = S2("vp", [128, 2, VSL], BF16)
            expT = S2("expT", [128, 2, 1024], BF16)
            oev = S2("oev", [128, 2, 512], F32)
            denb = S2("denb", [128, 8], F32)
            recb = S2("recb", [128, 8], F32)
            o_chunk = S2("o_chunk", [128, 2, 4, D], F32)
            xb2 = S2("xb2", [128, 2, D], F32)
            sqb2 = S2("sqb2", [128, 2, D], F32)
            stb2 = S2("stb2", [128, 4, 8], F32)
            lno = S2("lno", [128, 2, D], F32)
            xpb = S2("xpb", [128, 2, D], F32)
            h2b = S2("h2b", [128, 2, D], BF16)
            sc_ps = P2("sc_ps", [128, 2, 2, 512], F32)
            oP2 = P2("oP2", [128, 2, 512], F32)
            ttm = P2("ttm", [128, 8, 128], F32)

            bg = []

            def pump(n=1):
                for _ in range(n):
                    while bg:
                        try:
                            next(bg[0])
                            break
                        except StopIteration:
                            bg.pop(0)
                    else:
                        return

            def gen_ln(src, dst, sq, st, ch_sq, ch_var, ch_rstd, ch_out,
                       waits_src_act=(), waits_src_dve=(), extra_final_waits=()):
                p.emit("scalar",
                       lambda e, s=src, q=sq: e.activation(q, s, AF.Square),
                       waits=waits_src_act, inc=ch_sq)
                p.emit("vector",
                       lambda e, s=src, t=st: e.reduce_sum(t[:, 0:1], s, axis=AX.X),
                       waits=waits_src_dve)
                p.emit("vector",
                       lambda e, q=sq, t=st: e.reduce_sum(t[:, 1:2], q, axis=AX.X),
                       waits=[(ch_sq, ch_sq.n)])
                p.emit("vector",
                       lambda e, t=st: e.tensor_scalar(t[:, 2:3], t[:, 0:1], 1.0 / D, None, op0=ALU.mult))
                p.emit("vector",
                       lambda e, t=st: e.tensor_scalar(t[:, 3:4], t[:, 1:2], 1.0 / D, None, op0=ALU.mult))
                p.emit("vector",
                       lambda e, t=st: e.tensor_tensor(t[:, 4:5], t[:, 2:3], t[:, 2:3], op=ALU.mult))
                p.emit("vector",
                       lambda e, t=st: e.tensor_tensor(t[:, 5:6], t[:, 3:4], t[:, 4:5], op=ALU.subtract),
                       inc=ch_var)
                yield
                p.emit("scalar",
                       lambda e, t=st: e.activation(t[:, 6:7], t[:, 5:6], AF.Ln,
                                                    bias=epsb[0:t.shape[0], 0:1]),
                       waits=[(ch_var, ch_var.n), (c_cst, 1)])
                p.emit("scalar",
                       lambda e, t=st: e.activation(t[:, 7:8], t[:, 6:7], AF.Exp, scale=-0.5),
                       inc=ch_rstd)
                yield
                p.emit("vector",
                       lambda e, s=src, d=dst, t=st: e.tensor_scalar(
                           d, s, t[:, 2:3], t[:, 7:8], op0=ALU.subtract, op1=ALU.mult),
                       waits=[(ch_rstd, ch_rstd.n)] + list(extra_final_waits), inc=ch_out)

            res_done = [0] * NT
            rv_snap = [0, 0]
            rs_done_chunk = [0] * NCH

            def residual_gen(ch, div_all):
                for i in range(4):
                    tt = ch * 4 + i
                    par = tt % 2
                    wl = [(c_xp, res_done[tt - 2])] if tt >= 2 else []
                    p.emit("sync",
                           lambda e, t=tt, pr=par: e.dma_start(
                               xb2[:, pr, :], x_d[t * 128:(t + 1) * 128, :]),
                           waits=wl, inc=d_x2, dma=True)
                    yield
                    yield from gen_ln(o_chunk[:, ch % 2, i, :], lno[:, par, :],
                                      sqb2[:, par, :], stb2[:, par * 2, :],
                                      c_rs, c_rv, c_rr, c_lno,
                                      waits_src_act=[(c_div, div_all), (c_rv, rv_snap[par])],
                                      waits_src_dve=[(c_div, div_all)])
                    p.emit("vector",
                           lambda e, pr=par: e.tensor_tensor(
                               xpb[:, pr, :], xb2[:, pr, :], lno[:, pr, :], op=ALU.add),
                           waits=[(d_x2, d_x2.n), (d_xp, 16 * max(0, tt - 1))], inc=c_xp)
                    res_done[tt] = c_xp.n
                    p.emit("sync",
                           lambda e, t=tt, pr=par: e.dma_start(
                               xp_scr[t * 128:(t + 1) * 128, :], xpb[:, pr, :]),
                           waits=[(c_xp, c_xp.n)], inc=d_xp, dma=True)
                    yield
                    yield from gen_ln(xpb[:, par, :], h2b[:, par, :],
                                      sqb2[:, par, :], stb2[:, par * 2 + 1, :],
                                      c_rs, c_rv, c_rr, c_res,
                                      waits_src_act=[(c_xp, c_xp.n), (c_rv, rv_snap[par])],
                                      waits_src_dve=[(c_xp, c_xp.n)],
                                      extra_final_waits=[(d_h2, 16 * max(0, tt - 1))])
                    rv_snap[par] = c_rv.n
                    p.emit("sync",
                           lambda e, t=tt, pr=par: e.dma_start(
                               h2_scr[t * 128:(t + 1) * 128, :], h2b[:, pr, :]),
                           waits=[(c_res, c_res.n)], inc=d_h2, dma=True)
                    yield
                rs_done_chunk[ch] = c_rs.n

            for s2 in range(2):
                p.emit("vector",
                       lambda e, s=s2: e.memset(
                           vp[:, s, 0:NT * 130].rearrange(
                               "p (t h d) -> p t h d", t=NT, h=2)[:, :, :, 64:65],
                           1.0),
                       inc=c_cst)
                p.emit("vector",
                       lambda e, s=s2: e.memset(vp[:, s, NT * 130:VSL], 0.0),
                       inc=c_cst)
            for cc2 in range(2):
                p.emit("gpsimd",
                       lambda en, c2=cc2: en.dma_start(w1c[:, c2, :, :], wa1t_d[0, c2]),
                       inc=d_w13, dma=True)
                p.emit("gpsimd",
                       lambda en, c2=cc2: en.dma_start(w3c[:, c2, :, :], wa3t_d[0, c2]),
                       inc=d_w13, dma=True)

            kv_loaded = [0, 0]
            sc_snap_slot = [0, 0]
            av_snap_slot = [0, 0]
            qc_loaded = [0, 0]
            qc_sc_snap = [0, 0]
            exp_after_hist = [0, 0]
            av_hist = [0, 0]
            fin = {"oev": 0, "otr": 0, "div": 0}
            pending_fin = []

            def emit_kv_load(si, extra=()):
                ch2, pr2 = divmod(si, NP)
                sl = si % 2
                w = list(extra) + [(c_sc, sc_snap_slot[sl]),
                                   (d_qk, 16 * 2 * NP * NCH)]
                p.emit("sync",
                       lambda e, pi=pr2, s=sl: e.dma_start(kTp[:, s, :], kT_scr[pi]),
                       waits=w, inc=d_kv, dma=True)
                p.emit("sync",
                       lambda e, pi=pr2, s=sl: e.dma_start(
                           vp[:, s, 0:NT * 130].rearrange(
                               "p (t h d) -> p t h d", t=NT, h=2)[:, :, :, 0:64],
                           v_scr[pi]),
                       waits=[(d_v, 16 * NT * 2), (c_cst, 5),
                              (c_av, av_snap_slot[sl])],
                       inc=d_kv, dma=True)
                kv_loaded[sl] = d_kv.n

            emit_kv_load(0, extra=qkv_reads_done)

            for ch in range(NCH):
                cpar = ch % 2
                p.scope(f"att{ch}")
                if ch == 4:
                    for dt in range(8):
                        p.emit("sync",
                               lambda e, d=dt: e.dma_start_transpose(
                                   h2T_pre[:, d, :],
                                   h2_scr[0:QT, d * 128:(d + 1) * 128]),
                               waits=[(d_h2, 16 * 8)], inc=d_h2p, dma=True)
                qcw = [(d_qk, 16 * 2 * NP * NCH), (c_sc, qc_sc_snap[cpar])]
                if ch < 2:
                    qcw = qcw + list(qkv_reads_done)
                p.emit("gpsimd",
                       lambda e, c=ch, pr=cpar: e.dma_start(qTc[:, pr, :, :], qT_scr[c]),
                       waits=qcw, inc=d_qc, dma=True)
                qc_loaded[cpar] = d_qc.n
                for pr in range(NP):
                    si = ch * NP + pr
                    sl = si % 2
                    if si + 1 < NP * NCH:
                        emit_kv_load(si + 1,
                                     extra=qkv_reads_done if si == 0 else ())
                    my_kv = kv_loaded[sl]
                    av_base = c_av.n
                    exp_after = [0] * KT

                    def emit_scores(kt):
                        def sc_mms(e, k=kt, s=sl, pr2=pr, cp=cpar):
                            e.matmul(
                                sc_ps[:, k % 2, 0, :],
                                kTp[0:64, s, k * 128:(k + 1) * 128],
                                qTc[0:64, cp, pr2, :],
                                start=True, stop=True, tile_position=(0, 0),
                                skip_group_check=True)
                            return e.matmul(
                                sc_ps[:, k % 2, 1, :],
                                kTp[64:128, s, k * 128:(k + 1) * 128],
                                qTc[64:128, cp, pr2, :],
                                start=True, stop=True, tile_position=(64, 0),
                                skip_group_check=True)
                        scw = [(c_exp, exp_after[kt - 2] if kt >= 2 else exp_after_hist[kt % 2]),
                               (d_kv, my_kv), (d_qc, qc_loaded[cpar])]
                        if si == 0 and kt == 0:
                            scw = scw + list(qkv_reads_done)
                        p.emit("tensor", sc_mms, waits=scw, inc=c_sc)
                        p.emit("scalar",
                               lambda e, k=kt, pr2=pr: e.activation(
                                   expT[:, k % 2, :],
                                   sc_ps[:, k % 2, :, :].rearrange("p a b -> p (a b)"),
                                   AF.Exp, bias=biasg[:, pr2:pr2 + 1], scale=0.125),
                               waits=[(c_sc, c_sc.n), (c_mb, 8)],
                               inc=c_exp, nochain=True)
                        exp_after[kt] = c_exp.n

                    def emit_av(kt):
                        def av_mms(e, k=kt, s=sl):
                            off = k * 130
                            e.matmul(
                                oP2[:, 0, :], vp[:, s, off:off + 128],
                                expT[:, k % 2, 0:512],
                                start=(k == 0), stop=(k == KT - 1),
                                skip_group_check=True)
                            return e.matmul(
                                oP2[:, 1, :], vp[:, s, off + 65:off + 193],
                                expT[:, k % 2, 512:1024],
                                start=(k == 0), stop=(k == KT - 1),
                                skip_group_check=True)
                        wav = [(c_exp, exp_after[kt])]
                        if kt == 0:
                            wav.append((c_oev, fin["oev"]))
                        p.emit("tensor", av_mms, waits=wav, inc=c_av)
                        av_hist[kt % 2] = c_av.n

                    for kt in range(KT):
                        emit_scores(kt)
                        if kt == 0 and pending_fin:
                            pending_fin.pop(0)()
                        if kt >= 1:
                            emit_av(kt - 1)
                        pump(1)
                    emit_av(KT - 1)
                    exp_after_hist[0] = exp_after[KT - 2]
                    exp_after_hist[1] = exp_after[KT - 1]
                    sc_snap_slot[sl] = c_sc.n
                    av_snap_slot[sl] = c_av.n
                    qc_sc_snap[cpar] = c_sc.n

                    def make_finalize(si=si, pr=pr, cpar=cpar, ch=ch,
                                      av_base=av_base):
                        def fin_emit():
                            p.emit("vector",
                                   lambda e: e.tensor_copy(oev[0:65, :, :], oP2[0:65, :, :]),
                                   waits=[(c_av, av_base + KT), (c_otr, fin["otr"])],
                                   inc=c_oev)
                            fin["oev"] = c_oev.n

                            def otr_mms(e):
                                ins = None
                                for j in range(8):
                                    h, i = divmod(j, 4)
                                    ins = e.transpose(
                                        ttm[:, j, 0:65],
                                        oev[0:65, h, i * 128:(i + 1) * 128],
                                        id32[0:65, 0:65])
                                return ins
                            otw = [(c_oev, c_oev.n), (c_div, fin["div"]), (d_c, 16)]
                            if si == 0:
                                otw.extend(qkv_reads_done)
                            p.emit("tensor", otr_mms, waits=otw, inc=c_otr)
                            fin["otr"] = c_otr.n
                            p.emit("vector",
                                   lambda e: e.tensor_copy(
                                       denb[:], ttm[:, :, 64:65].rearrange("p a b -> p (a b)")),
                                   waits=[(c_otr, c_otr.n)])
                            p.emit("vector",
                                   lambda e: e.reciprocal(recb[:], denb[:]),
                                   inc=c_rcp)
                            dvw = [(c_rcp, c_rcp.n)]
                            if ch >= 2:
                                dvw.append((c_rs, rs_done_chunk[ch - 2]))
                            for j in range(8):
                                h, i = divmod(j, 4)
                                last = (j == 7)
                                p.emit("vector",
                                       lambda e, jj=j, ii=i, h2=h, pr2=pr, cp=cpar: e.tensor_scalar(
                                           o_chunk[:, cp, ii, pr2 * 128 + h2 * 64:pr2 * 128 + (h2 + 1) * 64],
                                           ttm[:, jj, 0:64],
                                           recb[:, jj:jj + 1], None,
                                           op0=ALU.mult),
                                       waits=dvw if j == 0 else [],
                                       inc=c_div if last else None)
                            fin["div"] = c_div.n
                        return fin_emit
                    pending_fin.append(make_finalize())
                    pump(2)
                while pending_fin:
                    pending_fin.pop(0)()
                bg.append(residual_gen(ch, c_div.n))
            while bg:
                pump(1)

            attn_done = [(c_sc, c_sc.n), (c_av, c_av.n),
                         (c_exp, c_exp.n), (c_div, c_div.n),
                         (c_rcp, c_rcp.n), (c_oev, c_oev.n),
                         (c_otr, c_otr.n), (c_lno, c_lno.n), (c_res, c_res.n),
                         (c_rs, c_rs.n), (c_rv, c_rv.n), (c_rr, c_rr.n),
                         (c_xp, c_xp.n),
                         (d_xp, d_xp.n), (d_h2, d_h2.n), (d_x2, d_x2.n),
                         (d_kv, d_kv.n), (d_qc, d_qc.n)]

        es3 = ExitStack()
        S3 = lambda n, sh, dt: es3.enter_context(nc.sbuf_tensor(n, sh, dt))
        P3 = lambda n, sh, dt: es3.enter_context(nc.psum_tensor(n, sh, dt))
        with es3:
            h2T = S3("h2T", [128, 8, QT], BF16)
            faq = S3("faq", [128, 8, QT], F32)
            zb = S3("zb", [128, 16, QT], BF16)
            zs = S3("zs", [128, QT], F32)
            w2d = S3("w2d", [128, 2, 16, 128], BF16)
            cwb = S3("cwb", [128, QT], F32)
            tmpf = S3("tmpf", [128, 512], F32)
            ctsb = S3("ctsb", [8, 128], F32)
            cwrow = S3("cwrow", [1, QT], F32)
            lgT = S3("lgT", [8, QT], F32)
            lgtm = S3("lgtm", [128, 8], F32)
            m1b = S3("m1b", [128, 4], F32)
            mskb = S3("mskb", [128, 8], F32)
            combtm = S3("combtm", [128, 8], F32)
            ftile = S3("ftile", [128, 2, D], F32)
            sqb3 = S3("sqb3", [128, 2, D], F32)
            stb3 = S3("stb3", [128, 2, 8], F32)
            lnf = S3("lnf", [128, 2, D], F32)
            xpl = S3("xpl", [128, 2, D], F32)
            outb = S3("outb", [128, 2, D], F32)
            z1_ps = P3("z1_ps", [128, 2, 512], F32)
            z3_ps = P3("z3_ps", [128, 2, 512], F32)
            y2_ps = P3("y2_ps", [128, 2, 512], F32)
            mi_ps = P3("mi_ps", [128, 2, 512], F32)



            psum_done = [(c_exp, c_exp.n), (c_oev, c_oev.n), (c_div, c_div.n)]

            w13_done = {}
            w2_done = {}
            z1_snap = [0, 0]
            z3_snap = [0, 0]
            y2_snap = [0, 0]
            z_reads = 0
            h2T_reads = 0
            fu_all = 0
            fsnap = {"ftr": 0, "ftc": 0}
            fv_snap = [0, 0]
            cw_read_snap = 0

            def emit_w13_load(q, e_i, c, extra=()):
                gi = q * NE + e_i
                need = gi * 16 + 4 * c - 4
                w = list(extra) + ([(c_z3, need)] if need > 0 else [])
                p.emit("gpsimd",
                       lambda en, ei=e_i, cc=c, pr=c % 2: en.dma_start(
                           w1c[:, pr, :, :], wa1t_d[ei, cc]),
                       waits=w, inc=d_w13, dma=True)
                p.emit("gpsimd",
                       lambda en, ei=e_i, cc=c, pr=c % 2: en.dma_start(
                           w3c[:, pr, :, :], wa3t_d[ei, cc]),
                       inc=d_w13, dma=True)
                w13_done[(q, e_i, c)] = d_w13.n

            def emit_w2_load(q, e_i, dt, extra=()):
                gi = q * NE + e_i
                need = gi * 16 + 2 * dt - 2
                w = list(extra) + ([(c_y2, need)] if need > 0 else [])
                if gi == 0 and dt < 2:
                    w = w + list(attn_done)
                p.emit("gpsimd",
                       lambda en, ei=e_i, d2=dt: en.dma_start(
                           w2d[:, d2 % 2, :, :], wa2t_d[ei, d2]),
                       waits=w, inc=d_w2, dma=True)
                w2_done[(q, e_i, dt)] = d_w2.n

            wload_seq = []
            for q in range(NQ):
                for e_i in range(NE):
                    for c in range(4):
                        wload_seq.append(("w13", q, e_i, c))
                    for dt in range(8):
                        wload_seq.append(("w2", q, e_i, dt))
            wl_idx = [0]

            def pump_wloads(n, extra=()):
                for _ in range(n):
                    if wl_idx[0] >= len(wload_seq):
                        return
                    item = wload_seq[wl_idx[0]]
                    wl_idx[0] += 1
                    if item[0] == "w2":
                        emit_w2_load(item[1], item[2], item[3], extra=extra)
                    else:
                        emit_w13_load(item[1], item[2], item[3], extra=extra)

            h2T_loaded = [None] * NQ
            h2T_read_snap = [0, 0, 0, 0]

            def h2T_tile(qq):
                return h2T_pre if qq % 2 == 0 else h2T

            h2T_loaded[0] = (d_h2p, d_h2p.n)

            def emit_h2T_load(qq):
                w = [(d_h2, 16 * 4 * (2 * qq + 2))]
                if qq == 1:
                    w += list(attn_done)
                elif qq >= 2:
                    w += [(c_z1, h2T_read_snap[qq - 2]), (c_lg, 2 * (qq - 1))]
                eng = "sync" if qq % 2 == 0 else "scalar"
                ch2 = d_h2p if qq % 2 == 0 else d_h2T
                dst = h2T_tile(qq)
                for dt in range(8):
                    p.emit(eng,
                           lambda e, d=dt, q2=qq, t=dst: e.dma_start_transpose(
                               t[:, d, :],
                               h2_scr[q2 * QT:(q2 + 1) * QT, d * 128:(d + 1) * 128]),
                           waits=w, inc=ch2, dma=True)
                h2T_loaded[qq] = (ch2, ch2.n)

            wl_idx[0] = 2
            w13_done[(0, 0, 0)] = 32
            w13_done[(0, 0, 1)] = 64
            pump_wloads(1)
            emit_h2T_load(1)
            pending_ffn = []
            for q in range(NQ):
                p.scope(f"ffq{q}")

                def lg_mms(e, half, qq):
                    ins = None
                    t = h2T_tile(qq)
                    for dt in range(8):
                        ins = e.matmul(mi_ps[0:8, half, :], rwt_sb[:, dt, :],
                                       t[:, dt, half * 512:(half + 1) * 512],
                                       start=(dt == 0), stop=(dt == 7))
                    return ins
                for half in range(2):
                    lgw = [(d_rw, 16), h2T_loaded[q],
                           (c_lgev, c_lgev.n), (c_ftc, fsnap["ftc"]), (c_cw, c_cw.n)]
                    if q == 0 and half == 0:
                        lgw += [(c_rcp, c_rcp.n), (c_div, c_div.n),
                                (c_av, c_av.n), (c_exp, c_exp.n),
                                (c_oev, c_oev.n), (c_otr, c_otr.n)]
                    p.emit("tensor",
                           lambda e, h=half, qq=q: lg_mms(e, h, qq),
                           waits=lgw, inc=c_lg)
                p.emit("vector",
                       lambda e: e.tensor_scalar(
                           lgT[:], mi_ps[0:8, :, :].rearrange("p a b -> p (a b)"),
                           ebias[:, 0:1], None, op0=ALU.add),
                       waits=[(c_lg, c_lg.n), (d_c, 64), (c_ltr, c_ltr.n)],
                       inc=c_lgev)
                lgev_now = c_lgev.n

                def make_tk(i, q, lgev_now=None):
                    def tk_emit():
                        p.emit("tensor",
                               lambda e, ii=i: e.transpose(
                                   mi_ps[:, 0, 0:8], lgT[0:8, ii * 128:(ii + 1) * 128],
                                   id32[0:8, 0:8]),
                               waits=[(c_lgev, lgev_now), (c_tk, c_tk.n),
                                      (c_ftc, c_ftc.n)],
                               inc=c_ltr)
                        p.emit("vector",
                               lambda e: e.tensor_copy(lgtm[:], mi_ps[:, 0, 0:8]),
                               waits=[(c_ltr, c_ltr.n)], inc=c_tk)
                        p.emit("vector", lambda e: e.reduce_max(m1b[:, 0:1], lgtm[:], axis=AX.X))
                        p.emit("vector",
                               lambda e: e.tensor_scalar(mskb[:], lgtm[:], m1b[:, 0:1], -1e30,
                                                         op0=ALU.is_ge, op1=ALU.mult))
                        p.emit("vector",
                               lambda e: e.tensor_tensor(mskb[:], lgtm[:], mskb[:], op=ALU.add))
                        p.emit("vector", lambda e: e.reduce_max(m1b[:, 1:2], mskb[:], axis=AX.X))
                        p.emit("vector",
                               lambda e: e.tensor_scalar(m1b[:, 2:3], m1b[:, 1:2], m1b[:, 0:1], None,
                                                         op0=ALU.subtract),
                               inc=c_tk)
                        p.emit("scalar",
                               lambda e: e.activation(m1b[:, 2:3], m1b[:, 2:3], AF.Exp),
                               waits=[(c_tk, c_tk.n)], inc=c_rr)
                        p.emit("vector",
                               lambda e: e.tensor_scalar(m1b[:, 2:3], m1b[:, 2:3], 1.0, None, op0=ALU.add),
                               waits=[(c_rr, c_rr.n)])
                        p.emit("vector", lambda e: e.reciprocal(m1b[:, 3:4], m1b[:, 2:3]))
                        p.emit("vector",
                               lambda e: e.tensor_scalar(combtm[:], lgtm[:], m1b[:, 0:1], None,
                                                         op0=ALU.subtract),
                               inc=c_tk)
                        p.emit("scalar",
                               lambda e: e.activation(combtm[:], combtm[:], AF.Exp),
                               waits=[(c_tk, c_tk.n)], inc=c_rr)
                        p.emit("vector",
                               lambda e: e.tensor_scalar(mskb[:], lgtm[:], m1b[:, 1:2], None,
                                                         op0=ALU.is_ge),
                               waits=[(c_rr, c_rr.n)])
                        p.emit("vector",
                               lambda e: e.tensor_tensor(combtm[:], combtm[:], mskb[:], op=ALU.mult))
                        p.emit("vector",
                               lambda e: e.tensor_scalar(combtm[:], combtm[:], m1b[:, 3:4], None,
                                                         op0=ALU.mult),
                               inc=c_tk)
                        p.emit("tensor",
                               lambda e: e.transpose(mi_ps[0:8, 1, 0:128], combtm[:], id32[:]),
                               waits=[(c_tk, c_tk.n), (c_cmb, c_cmb.n)], inc=c_ctr)
                        p.emit("vector",
                               lambda e: e.tensor_copy(ctsb[:], mi_ps[0:8, 1, 0:128]),
                               waits=[(c_ctr, c_ctr.n), (d_cws, d_cws.n)], inc=c_cmb)
                        p.emit("sync",
                               lambda e, t=q * 8 + i: e.dma_start(
                                   cwT_scr[:, t * 128:(t + 1) * 128], ctsb[:]),
                               waits=[(c_cmb, c_cmb.n)], inc=d_cws, dma=True)
                    return tk_emit
                for i in range(8):
                    pending_ffn.append(make_tk(i, q, lgev_now))

                for e_i in range(NE):
                    routed = e_i >= 2
                    if routed:
                        while pending_ffn:
                            pending_ffn.pop(0)()
                        p.emit("sync",
                               lambda e, ei=e_i, qq=q: e.dma_start(
                                   cwrow[:], cwT_scr[ei - 2:ei - 1, qq * QT:(qq + 1) * QT]),
                               waits=[(d_cws, 16 * 8 * (q + 1)), (c_cwm, c_cwm.n)],
                               inc=d_cwl, dma=True)
                        for half in range(2):
                            p.emit("tensor",
                                   lambda e, h=half: e.matmul(
                                       mi_ps[:, h, :], ones32[0:1, :],
                                       cwrow[0:1, h * 512:(h + 1) * 512],
                                       start=True, stop=True),
                                   waits=[(d_cwl, d_cwl.n), (c_ftc, fsnap["ftc"]),
                                          (c_ltr, c_ltr.n), (c_ctr, c_ctr.n),
                                          (c_cw, c_cw.n)],
                                   inc=c_cwm)
                        p.emit("vector",
                               lambda e: e.tensor_copy(
                                   cwb[:], mi_ps[:, :, :].rearrange("p a b -> p (a b)")),
                               waits=[(c_cwm, c_cwm.n), (c_fu, cw_read_snap)],
                               inc=c_cw)

                    for ht in range(16):
                        c = ht // 4
                        pump_wloads(1)
                        if pending_ffn and (e_i > 0 or ht >= 4):
                            pending_ffn.pop(0)()

                        def z_mms(e, w, ps, h=ht, cc=c, qq=q):
                            ins = None
                            t = h2T_tile(qq)
                            for half in range(2):
                                for dt in range(8):
                                    ins = e.matmul(
                                        ps[:, half, :],
                                        w[:, cc % 2, dt, (h % 4) * 128:(h % 4 + 1) * 128],
                                        t[:, dt, half * 512:(half + 1) * 512],
                                        start=(dt == 0), stop=(dt == 7),
                                        skip_group_check=True)
                            return ins
                        z1w = [(d_w13, w13_done[(q, e_i, c)]),
                               h2T_loaded[q],
                               (c_si, z1_snap[0])]
                        if q == 0 and e_i == 0 and ht == 0:
                            z1w += psum_done
                        p.emit("tensor",
                               lambda e, h=ht, cc=c, qq=q: z_mms(e, w1c, z1_ps, h, cc, qq),
                               waits=z1w,
                               inc=c_z1)
                        z1_snap[0] = c_si.n + 1
                        h2T_reads = c_z1.n
                        z3w = [(c_zm, z3_snap[0])]
                        if q == 0 and e_i == 0 and ht == 0:
                            z3w += psum_done
                        p.emit("tensor",
                               lambda e, h=ht, cc=c, qq=q: z_mms(e, w3c, z3_ps, h, cc, qq),
                               waits=z3w,
                               inc=c_z3)
                        z3_snap[0] = c_zm.n + 1
                        p.emit("scalar",
                               lambda e: e.activation(
                                   zs[:], z1_ps[:, :, :].rearrange("p a b -> p (a b)"), AF.Silu),
                               waits=[(c_z1, c_z1.n), (c_zm, c_zm.n)], inc=c_si)
                        p.emit("vector",
                               lambda e, h=ht: e.tensor_tensor(
                                   zb[:, h, :], zs[:],
                                   z3_ps[:, :, :].rearrange("p a b -> p (a b)"), op=ALU.mult),
                               waits=[(c_si, c_si.n), (c_z3, c_z3.n), (c_y2, z_reads)],
                               inc=c_zm)
                    zm_all = c_zm.n

                    for dt in range(8):
                        for half in range(2):
                            par = (dt * 2 + half) % 2

                            def y2_mms(e, d=dt, h=half, pr=par):
                                ins = None
                                for ht in range(16):
                                    ins = e.matmul(
                                        y2_ps[:, pr, :],
                                        w2d[:, d % 2, ht, :],
                                        zb[:, ht, h * 512:(h + 1) * 512],
                                        start=(ht == 0), stop=(ht == 15),
                                        skip_group_check=True)
                                return ins
                            y2w = [(c_zm, zm_all), (d_w2, w2_done[(q, e_i, dt)]),
                                   (c_fu, y2_snap[par])]
                            if q == 0 and e_i == 0 and dt == 0:
                                y2w += psum_done
                            p.emit("tensor", y2_mms, waits=y2w, inc=c_y2)
                            if e_i == 0:
                                p.emit("vector",
                                       lambda e, d=dt, h=half, pr=par: e.tensor_copy(
                                           faq[:, d, h * 512:(h + 1) * 512], y2_ps[:, pr, :]),
                                       waits=[(c_y2, c_y2.n), (c_ftr, fsnap["ftr"])],
                                       inc=c_fu)
                            elif not routed:
                                p.emit("vector",
                                       lambda e, d=dt, h=half, pr=par: e.scalar_tensor_tensor(
                                           faq[:, d, h * 512:(h + 1) * 512], y2_ps[:, pr, :],
                                           1.0, faq[:, d, h * 512:(h + 1) * 512],
                                           op0=ALU.mult, op1=ALU.add),
                                       waits=[(c_y2, c_y2.n)], inc=c_fu)
                            else:
                                p.emit("vector",
                                       lambda e, h=half, pr=par: e.tensor_tensor(
                                           tmpf[:], y2_ps[:, pr, :],
                                           cwb[:, h * 512:(h + 1) * 512], op=ALU.mult),
                                       waits=[(c_y2, c_y2.n), (c_cw, c_cw.n)])
                                p.emit("vector",
                                       lambda e, d=dt, h=half: e.tensor_tensor(
                                           faq[:, d, h * 512:(h + 1) * 512],
                                           faq[:, d, h * 512:(h + 1) * 512], tmpf[:], op=ALU.add),
                                       inc=c_fu)
                            y2_snap[par] = c_fu.n
                    z_reads = c_y2.n
                    w2_reads_snap = c_y2.n
                    cw_read_snap = c_fu.n
                fu_all = c_fu.n
                h2T_read_snap[q] = c_z1.n
                if q + 2 < NQ:
                    emit_h2T_load(q + 2)

                p.scope(f"fin{q}")
                oadd_done = [c_oadd.n, c_oadd.n]

                def make_qfin(i, q=q, fu_all_q=None, oadd_done=oadd_done):
                    tt = q * 8 + i
                    par = i % 2

                    def qfin_emit():
                        def mi8(dt):
                            return mi_ps[:, dt // 4, (dt % 4) * 128:(dt % 4 + 1) * 128]

                        def ftr_mms(e, ii=i, mi8=mi8):
                            ins = None
                            for dt in range(8):
                                ins = e.transpose(mi8(dt),
                                                  faq[:, dt, ii * 128:(ii + 1) * 128], id32[:])
                            return ins
                        p.emit("tensor", ftr_mms,
                               waits=[(c_fu, fu_all_q), (c_ftc, c_ftc.n),
                                      (c_cw, c_cw.n), (c_ltr, c_ltr.n),
                                      (c_lgev, c_lgev.n), (c_ctr, c_ctr.n),
                                      (c_cwm, c_cwm.n)],
                               inc=c_ftr)
                        fsnap["ftr"] = c_ftr.n
                        p.emit("vector",
                               lambda e, pr=par: e.tensor_copy(
                                   ftile[:, pr, :], mi_ps[:, :, :].rearrange("p a b -> p (a b)")),
                               waits=[(c_ftr, c_ftr.n)], inc=c_ftc)
                        fsnap["ftc"] = c_ftc.n
                        emit_ln(ftile[:, par, :], lnf[:, par, :], sqb3[:, par, :], stb3[:, par, :],
                                c_fs, c_fv, c_fr, c_fev,
                                waits_src_act=[(c_ftc, c_ftc.n), (c_fv, fv_snap[par])],
                                waits_src_dve=[(c_ftc, c_ftc.n)])
                        fv_snap[par] = c_fv.n
                        p.emit("sync",
                               lambda e, t=tt, pr=par: e.dma_start(
                                   xpl[:, pr, :], xp_scr[t * 128:(t + 1) * 128, :]),
                               waits=[(d_xp, 16 * (tt + 1)), (c_oadd, oadd_done[par])],
                               inc=d_xpl, dma=True)
                        p.emit("vector",
                               lambda e, pr=par: e.tensor_tensor(
                                   outb[:, pr, :], xpl[:, pr, :], lnf[:, pr, :], op=ALU.add),
                               waits=[(d_xpl, d_xpl.n), (d_out, 16 * max(0, tt - 1))],
                               inc=c_oadd)
                        oadd_done[par] = c_oadd.n
                        p.emit("sync",
                               lambda e, t=tt, pr=par: e.dma_start(
                                   out_d[t * 128:(t + 1) * 128, :], outb[:, pr, :]),
                               waits=[(c_oadd, c_oadd.n)], inc=d_out, dma=True)
                    return qfin_emit

                for i in range(8):
                    pending_ffn.append(make_qfin(i, fu_all_q=fu_all))

            while pending_ffn:
                pending_ffn.pop(0)()
            p.emit("sync", lambda e: e.nop(), waits=[(d_out, 16 * NT)])

        p.simulate()

        block = es0.enter_context(nc.Block())
        p.build2(block)

    return nc


_CACHE = {}


def _get_nc():
    if "nc" not in _CACHE:
        _CACHE["nc"] = build()
    return _CACHE["nc"]


def kernel(L0_attn_inputs, L0_attn_Wq, L0_attn_Wk, L0_attn_Wv,
           L0_ffn_router_w, L0_ffn_expert_bias, L0_ffn_w1, L0_ffn_w3,
           L0_ffn_w2, L0_ffn_sw1, L0_ffn_sw3, L0_ffn_sw2):
    nc = _get_nc()
    f32 = np.float32
    ac = np.ascontiguousarray
    x = np.asarray(L0_attn_inputs, dtype=f32)
    def _wqkv(w):
        return ac(np.asarray(w, f32).reshape(8, 128, D).transpose(1, 0, 2))
    wq, wk, wv = _wqkv(L0_attn_Wq), _wqkv(L0_attn_Wk), _wqkv(L0_attn_Wv)
    rwt = ac(np.asarray(L0_ffn_router_w, f32).T.reshape(8, 128, 8).transpose(1, 0, 2))
    eb = ac(np.asarray(L0_ffn_expert_bias, f32).reshape(8, 1))
    def _w13(sw, w):
        a = np.concatenate([np.asarray(sw, f32), np.asarray(w, f32)], 0).transpose(0, 2, 1)
        return ac(a.reshape(NE, 8, 128, 4, 512).transpose(0, 3, 2, 1, 4))
    wa1t = _w13(L0_ffn_sw1, L0_ffn_w1)
    wa3t = _w13(L0_ffn_sw3, L0_ffn_w3)
    a2 = np.concatenate([np.asarray(L0_ffn_sw2, f32), np.asarray(L0_ffn_w2, f32)], 0) \
           .transpose(0, 2, 1)
    wa2t = ac(a2.reshape(NE, 16, 128, 8, 128).transpose(0, 3, 2, 1, 4))
    id32 = np.eye(128, dtype=f32)
    onesbf = np.ones((128, 1), dtype=ml_dtypes.bfloat16)
    ones32 = np.ones((1, 128), dtype=f32)

    in_maps = []
    for c in range(8):
        in_maps.append({
            "x": ac(x[c]), "wq": wq, "wk": wk, "wv": wv,
            "rwt": rwt, "eb": eb, "wa1t": wa1t, "wa3t": wa3t, "wa2t": wa2t,
            "id32": id32, "onesbf": onesbf, "ones32": ones32,
        })
    res = run_bass_kernel_spmd(nc, in_maps, core_ids=list(range(8)))
    return np.stack([r["out"] for r in res.results], axis=0)


# version 15
# speedup vs baseline: 1.1487x; 1.1487x over previous
import numpy as np
import ml_dtypes
from contextlib import ExitStack

import concourse.bass as bass
import concourse.mybir as mybir
from concourse.bass_utils import run_bass_kernel_spmd

F32 = mybir.dt.float32
BF16 = mybir.dt.bfloat16
AF = mybir.ActivationFunctionType
ALU = mybir.AluOpType
AX = mybir.AxisListType

S = 4096
D = 1024
NT = S // 128
NCH = 8
KT = 32
NG = 4
NP = 8
HID = 2048
NE = 10
NQ = 4
QT = S // NQ
EPS = 1e-5

ENGINES = ("sync", "gpsimd", "tensor", "vector", "scalar")


class Chan:
    __slots__ = ("sem", "name", "n")

    def __init__(self, sem, name):
        self.sem = sem
        self.name = name
        self.n = 0


AUTOCHAIN = ("vector", "scalar")


class Prog:
    def __init__(self, nc, es):
        self.nc = nc
        self.es = es
        self.ops = {e: [] for e in ENGINES}
        self.cseq = {}
        self.last_marker = {}

    def init_autochain(self):
        for e in AUTOCHAIN:
            self.cseq[e] = self.chan(f"cseq_{e}")

    def chan(self, name):
        sem = self.es.enter_context(self.nc.semaphore(name))
        return Chan(sem, name)

    def scope(self, name):
        self.cur_scope = name

    def emit(self, eng, fn, waits=(), inc=None, dma=False, nochain=False):
        w = [(c, v) for (c, v) in waits if v > 0]
        if eng in self.cseq and not dma:
            lm = self.last_marker.get(eng)
            if lm is not None and not nochain:
                w.append(lm)
            if inc is None:
                inc = self.cseq[eng]
        self.ops[eng].append((w, fn, inc, dma, getattr(self, "cur_scope", "x")))
        if inc is not None:
            inc.n += 16 if dma else 1
            if eng in self.cseq and not dma:
                self.last_marker[eng] = (inc, inc.n)
        return inc.n if inc is not None else None

    def build(self, block):
        def mk(name):
            def body(eng):
                hwm = {}
                for w, fn, inc, dma in self.ops[name]:
                    for c, v in w:
                        if hwm.get(c.name, 0) < v:
                            eng.wait_ge(c.sem, v)
                            hwm[c.name] = v
                ins = fn(eng)
                if inc is not None:
                    ins.then_inc(inc.sem, 16 if dma else 1)
            return body

        raise RuntimeError("use build2")

    def build2(self, block):
        prog = self
        nc = self.nc

        def run_stream(name, eng):
            hwm = {}
            for w, fn, inc, dma, sc in prog.ops[name]:
                with nc.named_scope(sc):
                    for c, v in w:
                        if hwm.get(c.name, 0) < v:
                            eng.wait_ge(c.sem, v)
                            hwm[c.name] = v
                    ins = fn(eng)
                    if inc is not None:
                        ins.then_inc(inc.sem, 16 if dma else 1)

        @block.sync
        def _(e):
            run_stream("sync", e)

        @block.gpsimd
        def _(e):
            run_stream("gpsimd", e)

        @block.tensor
        def _(e):
            run_stream("tensor", e)

        @block.vector
        def _(e):
            run_stream("vector", e)

        @block.scalar
        def _(e):
            run_stream("scalar", e)

    def simulate(self):
        counts = {}
        idx = {e: 0 for e in ENGINES}
        total = sum(len(v) for v in self.ops.values())
        done = 0
        while done < total:
            progress = False
            for e in ENGINES:
                while idx[e] < len(self.ops[e]):
                    w, fn, inc, dma, _sc = self.ops[e][idx[e]]
                    if all(counts.get(c.name, 0) >= v for c, v in w):
                        if inc is not None:
                            counts[inc.name] = counts.get(inc.name, 0) + (16 if dma else 1)
                        idx[e] += 1
                        done += 1
                        progress = True
                    else:
                        break
            if not progress:
                blocked = {}
                for e in ENGINES:
                    if idx[e] < len(self.ops[e]):
                        w, _, _, _, _sc = self.ops[e][idx[e]]
                        blocked[e] = [(c.name, v, counts.get(c.name, 0)) for c, v in w
                                      if counts.get(c.name, 0) < v]
                raise RuntimeError(f"deadlock at op idx {idx}; blocked: {blocked}")


def build(dbg=False):
    nc = bass.Bass("TRN2", target_bir_lowering=False)

    x_d = nc.dram_tensor("x", [S, D], F32, kind="ExternalInput").ap()
    wq_d = nc.dram_tensor("wq", [8, 128, 8, 128], F32, kind="ExternalInput").ap()
    wk_d = nc.dram_tensor("wk", [8, 128, 8, 128], F32, kind="ExternalInput").ap()
    wv_d = nc.dram_tensor("wv", [2, 128, 8, 512], F32, kind="ExternalInput").ap()
    rwt_d = nc.dram_tensor("rwt", [128, 8, 8], F32, kind="ExternalInput").ap()
    eb_d = nc.dram_tensor("eb", [8, 1], F32, kind="ExternalInput").ap()
    wa1t_d = nc.dram_tensor("wa1t", [NE, 4, 128, 8, 512], F32, kind="ExternalInput").ap()
    wa3t_d = nc.dram_tensor("wa3t", [NE, 4, 128, 8, 512], F32, kind="ExternalInput").ap()
    wa2t_d = nc.dram_tensor("wa2t", [NE, 8, 128, 16, 128], F32, kind="ExternalInput").ap()
    id32_d = nc.dram_tensor("id32", [128, 128], F32, kind="ExternalInput").ap()
    onesbf_d = nc.dram_tensor("onesbf", [128, 1], BF16, kind="ExternalInput").ap()
    ones32_d = nc.dram_tensor("ones32", [1, 128], F32, kind="ExternalInput").ap()
    out_d = nc.dram_tensor("out", [S, D], F32, kind="ExternalOutput").ap()

    skind = "ExternalOutput" if dbg else "Internal"
    h_scr = nc.dram_tensor("h_scr", [S, D], BF16, kind=skind).ap()
    qT_scr = nc.dram_tensor("qT_scr", [NCH, 128, NP, 512], BF16, kind=skind).ap()
    kT_scr = nc.dram_tensor("kT_scr", [NP, 128, S], BF16, kind=skind).ap()
    v_scr = nc.dram_tensor("v_scr", [NP, 128, NT, 2, 64], BF16, kind=skind).ap()
    xp_scr = nc.dram_tensor("xp_scr", [S, D], F32, kind=skind).ap()
    h2_scr = nc.dram_tensor("h2_scr", [S, D], BF16, kind=skind).ap()
    cwT_scr = nc.dram_tensor("cwT_scr", [8, S], F32, kind=skind).ap()

    es0 = ExitStack()
    with es0:
        p = Prog(nc, es0)
        p.init_autochain()
        SB = lambda n, sh, dt: es0.enter_context(nc.sbuf_tensor(n, sh, dt))

        id32 = SB("id32s", [128, 128], F32)
        onesbf = SB("onesbfs", [128, 1], BF16)
        ones32 = SB("ones32s", [1, 128], F32)
        biasg = SB("biasg", [128, 8], F32)
        ebias = SB("ebiass", [8, 1], F32)
        epsb = SB("epsb", [128, 1], F32)
        rwt_sb = SB("rwt_sb", [128, 8, 8], BF16)

        d_c = p.chan("d_c")
        d_x = p.chan("d_x")
        d_h = p.chan("d_h")
        d_hT = p.chan("d_hT")
        d_wqkv = p.chan("d_wqkv")
        d_qk = p.chan("d_qk")
        d_v = p.chan("d_v")
        d_qc = p.chan("d_qc")
        d_h2p = p.chan("d_h2p")
        d_kv = p.chan("d_kv")
        d_x2 = p.chan("d_x2")
        d_xp = p.chan("d_xp")
        d_h2 = p.chan("d_h2")
        d_h2T = p.chan("d_h2T")
        d_w13 = p.chan("d_w13")
        d_w2 = p.chan("d_w2")
        d_rw = p.chan("d_rw")
        d_cws = p.chan("d_cws")
        d_cwl = p.chan("d_cwl")
        d_xpl = p.chan("d_xpl")
        d_out = p.chan("d_out")
        c_sq = p.chan("c_sq")
        c_var = p.chan("c_var")
        c_rstd = p.chan("c_rstd")
        c_ln = p.chan("c_ln")
        c_qk = p.chan("c_qk")
        c_qkev = p.chan("c_qkev")
        c_sq2 = p.chan("c_sq2")
        c_nrm = p.chan("c_nrm")
        c_nred = p.chan("c_nred")
        c_mbv = p.chan("c_mbv")
        c_mbb = p.chan("c_mbb")
        c_mb = p.chan("c_mb")
        c_v = p.chan("c_v")
        c_vev = p.chan("c_vev")
        c_sc = p.chan("c_sc")
        c_exp = p.chan("c_exp")
        c_av = p.chan("c_av")
        c_rcp = p.chan("c_rcp")
        c_oev = p.chan("c_oev")
        c_otr = p.chan("c_otr")
        c_div = p.chan("c_div")
        c_lno = p.chan("c_lno")
        c_xp = p.chan("c_xp")
        c_res = p.chan("c_res")
        c_rv = p.chan("c_rv")
        c_rr = p.chan("c_rr")
        c_rs = p.chan("c_rs")
        c_lg = p.chan("c_lg")
        c_lgev = p.chan("c_lgev")
        c_ltr = p.chan("c_ltr")
        c_tk = p.chan("c_tk")
        c_ctr = p.chan("c_ctr")
        c_cmb = p.chan("c_cmb")
        c_cw = p.chan("c_cw")
        c_cwm = p.chan("c_cwm")
        c_z1 = p.chan("c_z1")
        c_z3 = p.chan("c_z3")
        c_si = p.chan("c_si")
        c_zm = p.chan("c_zm")
        c_y2 = p.chan("c_y2")
        c_fu = p.chan("c_fu")
        c_ftr = p.chan("c_ftr")
        c_fev = p.chan("c_fev")
        c_ftc = p.chan("c_ftc")
        c_fv = p.chan("c_fv")
        c_fr = p.chan("c_fr")
        c_fs = p.chan("c_fs")
        c_oadd = p.chan("c_oadd")

        c_cst = p.chan("c_cst")
        h2T_pre = SB("h2T_pre", [128, 8, QT], BF16)
        w1c = SB("w1c", [128, 2, 8, 512], BF16)
        w3c = SB("w3c", [128, 2, 8, 512], BF16)
        p.scope("consts")
        p.emit("vector", lambda e: e.memset(epsb[:], EPS), inc=c_cst)
        p.emit("sync", lambda e, a=id32, b=id32_d: e.dma_start(a[:], b[:]), inc=d_c, dma=True)
        p.emit("sync", lambda e, a=onesbf, b=onesbf_d: e.dma_start(a[:], b[:]), inc=d_c, dma=True)
        p.emit("sync", lambda e, a=ones32, b=ones32_d: e.dma_start(a[:], b[:]), inc=d_c, dma=True)
        p.emit("sync", lambda e, a=ebias, b=eb_d: e.dma_start(a[:], b[:]), inc=d_c, dma=True)
        p.emit("gpsimd", lambda e: e.dma_start(rwt_sb[:], rwt_d[:]), inc=d_rw, dma=True)

        def emit_ln(src, dst, sq, st, ch_sq, ch_var, ch_rstd, ch_out,
                    waits_src_act=(), waits_src_dve=(), extra_final_waits=()):
            p.emit("vector",
                   lambda e, s=src, q=sq: e.tensor_tensor(q, s, s, op=ALU.mult),
                   waits=list(waits_src_act) + list(waits_src_dve), inc=ch_sq)
            p.emit("vector",
                   lambda e, s=src, t=st: e.reduce_sum(t[:, 0:1], s, axis=AX.X),
                   waits=waits_src_dve)
            p.emit("vector",
                   lambda e, q=sq, t=st: e.reduce_sum(t[:, 1:2], q, axis=AX.X),
                   waits=[(ch_sq, ch_sq.n)])
            p.emit("vector",
                   lambda e, t=st: e.tensor_scalar(t[:, 2:3], t[:, 0:1], 1.0 / D, None, op0=ALU.mult))
            p.emit("vector",
                   lambda e, t=st: e.tensor_scalar(t[:, 3:4], t[:, 1:2], 1.0 / D, None, op0=ALU.mult))
            p.emit("vector",
                   lambda e, t=st: e.tensor_tensor(t[:, 4:5], t[:, 2:3], t[:, 2:3], op=ALU.mult))
            p.emit("vector",
                   lambda e, t=st: e.tensor_tensor(t[:, 5:6], t[:, 3:4], t[:, 4:5], op=ALU.subtract),
                   inc=ch_var)
            p.emit("scalar",
                   lambda e, t=st: e.activation(t[:, 6:7], t[:, 5:6], AF.Sqrt,
                                                bias=epsb[0:t.shape[0], 0:1]),
                   waits=[(ch_var, ch_var.n), (c_cst, 1)], inc=ch_rstd)
            p.emit("vector",
                   lambda e, t=st: e.reciprocal(t[:, 7:8], t[:, 6:7]),
                   waits=[(ch_rstd, ch_rstd.n)], inc=ch_rstd)
            p.emit("vector",
                   lambda e, s=src, d=dst, t=st: e.tensor_scalar(
                       d, s, t[:, 2:3], t[:, 7:8], op0=ALU.subtract, op1=ALU.mult),
                   waits=[(ch_rstd, ch_rstd.n)] + list(extra_final_waits), inc=ch_out)

        es1 = ExitStack()
        S1 = lambda n, sh, dt: es1.enter_context(nc.sbuf_tensor(n, sh, dt))
        P1 = lambda n, sh, dt: es1.enter_context(nc.psum_tensor(n, sh, dt))
        with es1:
            xb = S1("xb", [128, 2, D], F32)
            sqb = S1("sqb", [128, 2, D], F32)
            stb = S1("stb", [128, 2, 8], F32)
            hb = S1("hb", [128, 2, D], BF16)
            hT = S1("hT", [128, 8, S], BF16)
            wq_sb = S1("wq_sb", [128, 8, 8, 128], BF16)
            wk_sb = S1("wk_sb", [128, 8, 8, 128], BF16)
            wv_sb = S1("wv_sb", [128, 2, 8, 512], BF16)
            ev_qk = S1("ev_qk", [128, 2, 512], BF16)
            sq2 = S1("sq2", [128, 2, 512], BF16)
            nmax = S1("nmax", [1, 256], F32)
            mbh = S1("mbh", [1, 64], F32)
            ev_v = S1("ev_v", [128, 2, 512], BF16)
            pq = P1("pq", [128, 4, 512], F32)
            pnrm = P1("pnrm", [128, 2, 512], F32)
            pbc = P1("pbc", [128, 512], F32)

            for pi2 in range(8):
                for w_sb, w_d in ((wq_sb, wq_d), (wk_sb, wk_d)):
                    p.emit("gpsimd",
                           lambda e, a=w_sb, b=w_d, i=pi2: e.dma_start(
                               a[:, i, :, :], b[i]),
                           inc=d_wqkv, dma=True)
            for hh in range(2):
                p.emit("gpsimd",
                       lambda e, h=hh: e.dma_start(wv_sb[:, h, :, :], wv_d[h]),
                       inc=d_wqkv, dma=True)

            qk_store_snap = [0, 0]
            v_store_snap = [0, 0]
            nrm_snap_par = [0, 0]
            nred_snap = [0]
            pq_free = [None, None, None, None]
            ln_done = [0] * NT
            sq_done = [0] * NT
            qkseq = [0]
            p.scope("lnqkv")

            def emit_ln_tile(tt):
                par = tt % 2
                wl = []
                if tt >= 2:
                    wl = [(c_ln, ln_done[tt - 2]), (c_sq, sq_done[tt - 2])]
                p.emit("sync",
                       lambda e, a=xb, pr=par, t=tt: e.dma_start(
                           a[:, pr, :], x_d[t * 128:(t + 1) * 128, :]),
                       waits=wl, inc=d_x, dma=True)
                hwait = [(d_h, 16 * (tt - 1))] if tt >= 2 else []
                emit_ln(xb[:, par, :], hb[:, par, :], sqb[:, par, :], stb[:, par, :],
                        c_sq, c_var, c_rstd, c_ln,
                        waits_src_act=[(d_x, d_x.n)],
                        waits_src_dve=[(d_x, d_x.n)],
                        extra_final_waits=hwait)
                sq_done[tt] = c_sq.n
                ln_done[tt] = c_ln.n
                p.emit("sync",
                       lambda e, a=hb, pr=par, t=tt: e.dma_start(
                           h_scr[t * 128:(t + 1) * 128, :], a[:, pr, :]),
                       waits=[(c_ln, c_ln.n)], inc=d_h, dma=True)
                for dt in range(8):
                    p.emit("scalar",
                           lambda e, d=dt, t=tt: e.dma_start_transpose(
                               hT[:, d, t * 128:(t + 1) * 128],
                               h_scr[t * 128:(t + 1) * 128, d * 128:(d + 1) * 128]),
                           waits=[(d_h, d_h.n)], inc=d_hT, dma=True)

            for tt in range(4):
                emit_ln_tile(tt)
            pending_nrm = []
            for ch in range(NCH):
                lnq = []
                if ch + 1 < NCH:
                    for tt in range(4 * ch + 4, 4 * ch + 8):
                        lnq.append(lambda t=tt: emit_ln_tile(t))

                p.scope(f"qk{ch}")
                for pr_i in range(NP):
                    for proj in range(2):
                        w_sb = (wq_sb, wk_sb)[proj]
                        par = qkseq[0] % 2
                        p4 = qkseq[0] % 4
                        qkseq[0] += 1

                        def qk_mms(e, w=w_sb, pi=pr_i, c=ch, pr=p4):
                            ins = None
                            for kt in range(8):
                                ins = e.matmul(pq[:, pr, :], w[:, pi, kt, :],
                                               hT[:, kt, c * 512:(c + 1) * 512],
                                               start=(kt == 0), stop=(kt == 7))
                            return ins
                        qw = [(d_wqkv, 16 * (2 * pr_i + 1 + proj)),
                              (d_hT, 16 * 32 * (ch + 1))]
                        if pq_free[p4] is not None:
                            qw.append(pq_free[p4])
                        p.emit("tensor", qk_mms, waits=qw, inc=c_qk)
                        if len(pending_nrm) >= 2:
                            pending_nrm.pop(0)()
                        if lnq and qkseq[0] % 3 == 2:
                            lnq.pop(0)()
                        p.emit("vector",
                               lambda e, pr=par, pp=p4: e.tensor_copy(
                                   ev_qk[:, pr, :], pq[:, pp, :]),
                               waits=[(c_qk, c_qk.n), (d_qk, qk_store_snap[par])],
                               inc=c_qkev)
                        pq_free[p4] = (c_qkev, c_qkev.n)
                        p.emit("vector",
                               lambda e, pr=par: e.tensor_tensor(
                                   sq2[:, pr, :], ev_qk[:, pr, :], ev_qk[:, pr, :], op=ALU.mult),
                               waits=[(c_nrm, nrm_snap_par[par])],
                               inc=c_sq2)

                        def nrm_mms(e, pr=par):
                            e.matmul(pnrm[0:1, 0, :], onesbf[0:64, 0:1], sq2[0:64, pr, :],
                                     start=True, stop=True, tile_position=(0, 0),
                                     skip_group_check=True)
                            return e.matmul(pnrm[0:1, 1, :], onesbf[64:128, 0:1],
                                            sq2[64:128, pr, :],
                                            start=True, stop=True, tile_position=(64, 0),
                                            skip_group_check=True)

                        def emit_nrm(par=par, sq2_cnt=c_sq2.n, nrm_fn=nrm_mms,
                                     idx2=((pr_i * 2 + proj) * 8 + ch) * 2):
                            p.emit("tensor", nrm_fn,
                                   waits=[(c_sq2, sq2_cnt), (c_nred, nred_snap[0]),
                                          (d_c, 32)],
                                   inc=c_nrm)
                            nrm_snap_par[par] = c_nrm.n
                            p.emit("vector",
                                   lambda e, i2=idx2: e.reduce_max(
                                       nmax[0:1, i2:i2 + 2], pnrm[0:1, 0:2, :],
                                       axis=AX.X),
                                   waits=[(c_nrm, c_nrm.n)], inc=c_nred)
                            nred_snap[0] = c_nred.n
                        pending_nrm.append(emit_nrm)
                        if proj == 0:
                            p.emit("sync",
                                   lambda e, pi=pr_i, c=ch, pr=par: e.dma_start(
                                       qT_scr[c, :, pi, :], ev_qk[:, pr, :]),
                                   waits=[(c_sq2, c_sq2.n)], inc=d_qk, dma=True)
                        else:
                            p.emit("sync",
                                   lambda e, pi=pr_i, c=ch, pr=par: e.dma_start(
                                       kT_scr[pi, :, c * 512:(c + 1) * 512], ev_qk[:, pr, :]),
                                   waits=[(c_sq2, c_sq2.n)], inc=d_qk, dma=True)
                        qk_store_snap[par] = d_qk.n

                p.scope(f"v{ch}")
                while pending_nrm:
                    pending_nrm.pop(0)()
                while lnq:
                    lnq.pop(0)()
                for tt in range(4 * ch, 4 * ch + 4):
                    for half in range(2):
                        par = qkseq[0] % 2
                        p4 = qkseq[0] % 4
                        qkseq[0] += 1

                        def v_mms(e, t=tt, h=half, pr=p4):
                            ins = None
                            for kt in range(8):
                                ins = e.matmul(pq[:, pr, :], hT[:, kt, t * 128:(t + 1) * 128],
                                               wv_sb[:, h, kt, :],
                                               start=(kt == 0), stop=(kt == 7))
                            return ins
                        vw = [(d_wqkv, 16 * (17 + half)), (d_hT, 16 * 32 * (ch + 1))]
                        if pq_free[p4] is not None:
                            vw.append(pq_free[p4])
                        p.emit("tensor", v_mms, waits=vw, inc=c_v)
                        p.emit("vector",
                               lambda e, pr=par, pp=p4: e.tensor_copy(
                                   ev_v[:, pr, :], pq[:, pp, :]),
                               waits=[(c_v, c_v.n), (d_v, v_store_snap[par])], inc=c_vev)
                        pq_free[p4] = (c_vev, c_vev.n)
                        p.emit("sync",
                               lambda e, t=tt, h=half, pr=par: e.dma_start(
                                   v_scr[h * 4:(h + 1) * 4, :, t, :, :].rearrange(
                                       "b p h d -> p b h d"),
                                   ev_v[:, pr, :].rearrange("p (b h d) -> p b h d",
                                                            b=4, h=2)),
                               waits=[(c_vev, c_vev.n)], inc=d_v, dma=True)
                        v_store_snap[par] = d_v.n

            p.scope("mbound")
            for pi in range(NP):
                for hl in range(2):
                    ms = pi * 8 + hl * 4
                    bq = pi * 32 + hl
                    bk = pi * 32 + 16 + hl
                    p.emit("vector",
                           lambda e, m=ms, b=bq: e.reduce_max(
                               mbh[0:1, m:m + 1], nmax[0:1, b:b + 15:2], axis=AX.X),
                           waits=[(c_nred, c_nred.n)])
                    p.emit("vector",
                           lambda e, m=ms, b=bk: e.reduce_max(
                               mbh[0:1, m + 1:m + 2], nmax[0:1, b:b + 15:2], axis=AX.X))
                    p.emit("vector",
                           lambda e, m=ms: e.tensor_tensor(
                               mbh[0:1, m + 2:m + 3], mbh[0:1, m:m + 1],
                               mbh[0:1, m + 1:m + 2], op=ALU.mult),
                           inc=c_mbv)
                    p.emit("scalar",
                           lambda e, m=ms: e.activation(
                               mbh[0:1, m + 2:m + 3], mbh[0:1, m + 2:m + 3], AF.Ln),
                           waits=[(c_mbv, c_mbv.n)])
                    p.emit("scalar",
                           lambda e, m=ms: e.activation(
                               mbh[0:1, m + 2:m + 3], mbh[0:1, m + 2:m + 3], AF.Exp, scale=0.5),
                           inc=c_rstd)
                    p.emit("vector",
                           lambda e, m=ms: e.tensor_scalar(
                               mbh[0:1, m + 3:m + 4], mbh[0:1, m + 2:m + 3],
                               -0.125, -1.0, op0=ALU.mult, op1=ALU.add),
                           waits=[(c_rstd, c_rstd.n)], inc=c_mbv)
                p.emit("vector",
                       lambda e, m=pi * 8: e.tensor_reduce(
                           mbh[0:1, m + 3:m + 4], mbh[0:1, m + 3:m + 8:4], axis=AX.X,
                           op=ALU.min),
                       inc=c_mbv)
                p.emit("tensor",
                       lambda e, m=pi * 8: e.matmul(pbc[:, 0:1], ones32[0:1, :],
                                                    mbh[0:1, m + 3:m + 4],
                                                    start=True, stop=True),
                       waits=[(c_mbv, c_mbv.n), (d_c, 48), (c_mb, c_mb.n)],
                       inc=c_mbb)
                p.emit("vector",
                       lambda e, g=pi: e.tensor_copy(biasg[:, g:g + 1], pbc[:, 0:1]),
                       waits=[(c_mbb, c_mbb.n)], inc=c_mb)

            qkv_reads_done = [(c_qk, c_qk.n), (c_v, c_v.n), (c_nrm, c_nrm.n),
                              (c_sq2, c_sq2.n), (c_qkev, c_qkev.n), (c_vev, c_vev.n),
                              (c_nred, c_nred.n), (c_mbv, c_mbv.n), (c_mb, c_mb.n),
                              (d_qk, d_qk.n), (d_v, d_v.n), (d_h, d_h.n),
                              (d_hT, d_hT.n), (d_x, d_x.n)]

        es2 = ExitStack()
        S2 = lambda n, sh, dt: es2.enter_context(nc.sbuf_tensor(n, sh, dt))
        P2 = lambda n, sh, dt: es2.enter_context(nc.psum_tensor(n, sh, dt))
        with es2:
            qTc = S2("qTc", [128, 2, 8, 512], BF16)
            kTp = S2("kTp", [128, 2, S], BF16)
            VSL = NT * 130 + 64
            vp = S2("vp", [128, 2, VSL], BF16)
            expT = S2("expT", [128, 2, 1024], BF16)
            oev = S2("oev", [128, 2, 512], F32)
            denb = S2("denb", [128, 8], F32)
            recb = S2("recb", [128, 8], F32)
            o_chunk = S2("o_chunk", [128, 2, 4, D], F32)
            xb2 = S2("xb2", [128, 2, D], F32)
            sqb2 = S2("sqb2", [128, 2, D], F32)
            stb2 = S2("stb2", [128, 4, 8], F32)
            lno = S2("lno", [128, 2, D], F32)
            xpb = S2("xpb", [128, 2, D], F32)
            h2b = S2("h2b", [128, 2, D], BF16)
            sc_ps = P2("sc_ps", [128, 2, 2, 512], F32)
            oP2 = P2("oP2", [128, 2, 512], F32)
            ttm = P2("ttm", [128, 8, 128], F32)

            bg = []

            def pump(n=1):
                for _ in range(n):
                    while bg:
                        try:
                            next(bg[0])
                            break
                        except StopIteration:
                            bg.pop(0)
                    else:
                        return

            def gen_ln(src, dst, sq, st, ch_sq, ch_var, ch_rstd, ch_out,
                       waits_src_act=(), waits_src_dve=(), extra_final_waits=()):
                p.emit("vector",
                       lambda e, s=src, q=sq: e.tensor_tensor(q, s, s, op=ALU.mult),
                       waits=list(waits_src_act) + list(waits_src_dve), inc=ch_sq)
                p.emit("vector",
                       lambda e, s=src, t=st: e.reduce_sum(t[:, 0:1], s, axis=AX.X),
                       waits=waits_src_dve)
                p.emit("vector",
                       lambda e, q=sq, t=st: e.reduce_sum(t[:, 1:2], q, axis=AX.X),
                       waits=[(ch_sq, ch_sq.n)])
                p.emit("vector",
                       lambda e, t=st: e.tensor_scalar(t[:, 2:3], t[:, 0:1], 1.0 / D, None, op0=ALU.mult))
                p.emit("vector",
                       lambda e, t=st: e.tensor_scalar(t[:, 3:4], t[:, 1:2], 1.0 / D, None, op0=ALU.mult))
                p.emit("vector",
                       lambda e, t=st: e.tensor_tensor(t[:, 4:5], t[:, 2:3], t[:, 2:3], op=ALU.mult))
                p.emit("vector",
                       lambda e, t=st: e.tensor_tensor(t[:, 5:6], t[:, 3:4], t[:, 4:5], op=ALU.subtract),
                       inc=ch_var)
                yield
                p.emit("scalar",
                       lambda e, t=st: e.activation(t[:, 6:7], t[:, 5:6], AF.Sqrt,
                                                    bias=epsb[0:t.shape[0], 0:1]),
                       waits=[(ch_var, ch_var.n), (c_cst, 1)], inc=ch_rstd)
                p.emit("vector",
                       lambda e, t=st: e.reciprocal(t[:, 7:8], t[:, 6:7]),
                       waits=[(ch_rstd, ch_rstd.n)], inc=ch_rstd)
                yield
                p.emit("vector",
                       lambda e, s=src, d=dst, t=st: e.tensor_scalar(
                           d, s, t[:, 2:3], t[:, 7:8], op0=ALU.subtract, op1=ALU.mult),
                       waits=[(ch_rstd, ch_rstd.n)] + list(extra_final_waits), inc=ch_out)

            res_done = [0] * NT
            rv_snap = [0, 0]
            rs_done_chunk = [0] * NCH

            def residual_gen(ch, div_all):
                for i in range(4):
                    tt = ch * 4 + i
                    par = tt % 2
                    wl = [(c_xp, res_done[tt - 2])] if tt >= 2 else []
                    p.emit("sync",
                           lambda e, t=tt, pr=par: e.dma_start(
                               xb2[:, pr, :], x_d[t * 128:(t + 1) * 128, :]),
                           waits=wl, inc=d_x2, dma=True)
                    yield
                    yield from gen_ln(o_chunk[:, ch % 2, i, :], lno[:, par, :],
                                      sqb2[:, par, :], stb2[:, par * 2, :],
                                      c_rs, c_rv, c_rr, c_lno,
                                      waits_src_act=[(c_div, div_all), (c_rv, rv_snap[par])],
                                      waits_src_dve=[(c_div, div_all)])
                    p.emit("vector",
                           lambda e, pr=par: e.tensor_tensor(
                               xpb[:, pr, :], xb2[:, pr, :], lno[:, pr, :], op=ALU.add),
                           waits=[(d_x2, d_x2.n), (d_xp, 16 * max(0, tt - 1))], inc=c_xp)
                    res_done[tt] = c_xp.n
                    p.emit("sync",
                           lambda e, t=tt, pr=par: e.dma_start(
                               xp_scr[t * 128:(t + 1) * 128, :], xpb[:, pr, :]),
                           waits=[(c_xp, c_xp.n)], inc=d_xp, dma=True)
                    yield
                    yield from gen_ln(xpb[:, par, :], h2b[:, par, :],
                                      sqb2[:, par, :], stb2[:, par * 2 + 1, :],
                                      c_rs, c_rv, c_rr, c_res,
                                      waits_src_act=[(c_xp, c_xp.n), (c_rv, rv_snap[par])],
                                      waits_src_dve=[(c_xp, c_xp.n)],
                                      extra_final_waits=[(d_h2, 16 * max(0, tt - 1))])
                    rv_snap[par] = c_rv.n
                    p.emit("sync",
                           lambda e, t=tt, pr=par: e.dma_start(
                               h2_scr[t * 128:(t + 1) * 128, :], h2b[:, pr, :]),
                           waits=[(c_res, c_res.n)], inc=d_h2, dma=True)
                    yield
                rs_done_chunk[ch] = c_rs.n

            for s2 in range(2):
                p.emit("vector",
                       lambda e, s=s2: e.memset(
                           vp[:, s, 0:NT * 130].rearrange(
                               "p (t h d) -> p t h d", t=NT, h=2)[:, :, :, 64:65],
                           1.0),
                       inc=c_cst)
                p.emit("vector",
                       lambda e, s=s2: e.memset(vp[:, s, NT * 130:VSL], 0.0),
                       inc=c_cst)
            for cc2 in range(2):
                p.emit("gpsimd",
                       lambda en, c2=cc2: en.dma_start(w1c[:, c2, :, :], wa1t_d[0, c2]),
                       inc=d_w13, dma=True)
                p.emit("gpsimd",
                       lambda en, c2=cc2: en.dma_start(w3c[:, c2, :, :], wa3t_d[0, c2]),
                       inc=d_w13, dma=True)

            kv_loaded = [0, 0]
            sc_snap_slot = [0, 0]
            av_snap_slot = [0, 0]
            qc_loaded = [0, 0]
            qc_sc_snap = [0, 0]
            exp_after_hist = [0, 0]
            av_hist = [0, 0]
            fin = {"oev": 0, "otr": 0, "div": 0}
            pending_fin = []

            def emit_kv_load(si, extra=()):
                ch2, pr2 = divmod(si, NP)
                sl = si % 2
                w = list(extra) + [(c_sc, sc_snap_slot[sl]),
                                   (d_qk, 16 * 2 * NP * NCH)]
                p.emit("sync",
                       lambda e, pi=pr2, s=sl: e.dma_start(kTp[:, s, :], kT_scr[pi]),
                       waits=w, inc=d_kv, dma=True)
                p.emit("sync",
                       lambda e, pi=pr2, s=sl: e.dma_start(
                           vp[:, s, 0:NT * 130].rearrange(
                               "p (t h d) -> p t h d", t=NT, h=2)[:, :, :, 0:64],
                           v_scr[pi]),
                       waits=[(d_v, 16 * NT * 2), (c_cst, 5),
                              (c_av, av_snap_slot[sl])],
                       inc=d_kv, dma=True)
                kv_loaded[sl] = d_kv.n

            emit_kv_load(0, extra=qkv_reads_done)

            for ch in range(NCH):
                cpar = ch % 2
                p.scope(f"att{ch}")
                if ch == 4:
                    for dt in range(8):
                        p.emit("sync",
                               lambda e, d=dt: e.dma_start_transpose(
                                   h2T_pre[:, d, :],
                                   h2_scr[0:QT, d * 128:(d + 1) * 128]),
                               waits=[(d_h2, 16 * 8)], inc=d_h2p, dma=True)
                qcw = [(d_qk, 16 * 2 * NP * NCH), (c_sc, qc_sc_snap[cpar])]
                if ch < 2:
                    qcw = qcw + list(qkv_reads_done)
                p.emit("gpsimd",
                       lambda e, c=ch, pr=cpar: e.dma_start(qTc[:, pr, :, :], qT_scr[c]),
                       waits=qcw, inc=d_qc, dma=True)
                qc_loaded[cpar] = d_qc.n
                for pr in range(NP):
                    si = ch * NP + pr
                    sl = si % 2
                    if si + 1 < NP * NCH:
                        emit_kv_load(si + 1,
                                     extra=qkv_reads_done if si == 0 else ())
                    my_kv = kv_loaded[sl]
                    av_base = c_av.n
                    exp_after = [0] * KT

                    def emit_scores(kt):
                        def sc_mms(e, k=kt, s=sl, pr2=pr, cp=cpar):
                            e.matmul(
                                sc_ps[:, k % 2, 0, :],
                                kTp[0:64, s, k * 128:(k + 1) * 128],
                                qTc[0:64, cp, pr2, :],
                                start=True, stop=True, tile_position=(0, 0),
                                skip_group_check=True)
                            return e.matmul(
                                sc_ps[:, k % 2, 1, :],
                                kTp[64:128, s, k * 128:(k + 1) * 128],
                                qTc[64:128, cp, pr2, :],
                                start=True, stop=True, tile_position=(64, 0),
                                skip_group_check=True)
                        scw = [(c_exp, exp_after[kt - 2] if kt >= 2 else exp_after_hist[kt % 2]),
                               (d_kv, my_kv), (d_qc, qc_loaded[cpar])]
                        if si == 0 and kt == 0:
                            scw = scw + list(qkv_reads_done)
                        p.emit("tensor", sc_mms, waits=scw, inc=c_sc)
                        p.emit("scalar",
                               lambda e, k=kt, pr2=pr: e.activation(
                                   expT[:, k % 2, :],
                                   sc_ps[:, k % 2, :, :].rearrange("p a b -> p (a b)"),
                                   AF.Exp, bias=biasg[:, pr2:pr2 + 1], scale=0.125),
                               waits=[(c_sc, c_sc.n), (c_mb, 8)],
                               inc=c_exp, nochain=True)
                        exp_after[kt] = c_exp.n

                    def emit_av(kt):
                        def av_mms(e, k=kt, s=sl):
                            off = k * 130
                            e.matmul(
                                oP2[:, 0, :], vp[:, s, off:off + 128],
                                expT[:, k % 2, 0:512],
                                start=(k == 0), stop=(k == KT - 1),
                                skip_group_check=True)
                            return e.matmul(
                                oP2[:, 1, :], vp[:, s, off + 65:off + 193],
                                expT[:, k % 2, 512:1024],
                                start=(k == 0), stop=(k == KT - 1),
                                skip_group_check=True)
                        wav = [(c_exp, exp_after[kt])]
                        if kt == 0:
                            wav.append((c_oev, fin["oev"]))
                        p.emit("tensor", av_mms, waits=wav, inc=c_av)
                        av_hist[kt % 2] = c_av.n

                    for kt in range(KT):
                        emit_scores(kt)
                        if kt == 0 and pending_fin:
                            pending_fin.pop(0)()
                        if kt >= 1:
                            emit_av(kt - 1)
                        pump(1)
                    emit_av(KT - 1)
                    exp_after_hist[0] = exp_after[KT - 2]
                    exp_after_hist[1] = exp_after[KT - 1]
                    sc_snap_slot[sl] = c_sc.n
                    av_snap_slot[sl] = c_av.n
                    qc_sc_snap[cpar] = c_sc.n

                    def make_finalize(si=si, pr=pr, cpar=cpar, ch=ch,
                                      av_base=av_base):
                        def fin_emit():
                            p.emit("vector",
                                   lambda e: e.tensor_copy(oev[0:65, :, :], oP2[0:65, :, :]),
                                   waits=[(c_av, av_base + KT), (c_otr, fin["otr"])],
                                   inc=c_oev)
                            fin["oev"] = c_oev.n

                            def otr_mms(e):
                                ins = None
                                for j in range(8):
                                    h, i = divmod(j, 4)
                                    ins = e.transpose(
                                        ttm[:, j, 0:65],
                                        oev[0:65, h, i * 128:(i + 1) * 128],
                                        id32[0:65, 0:65])
                                return ins
                            otw = [(c_oev, c_oev.n), (c_div, fin["div"]), (d_c, 16)]
                            if si == 0:
                                otw.extend(qkv_reads_done)
                            p.emit("tensor", otr_mms, waits=otw, inc=c_otr)
                            fin["otr"] = c_otr.n
                            p.emit("vector",
                                   lambda e: e.tensor_copy(
                                       denb[:], ttm[:, :, 64:65].rearrange("p a b -> p (a b)")),
                                   waits=[(c_otr, c_otr.n)])
                            p.emit("vector",
                                   lambda e: e.reciprocal(recb[:], denb[:]),
                                   inc=c_rcp)
                            dvw = [(c_rcp, c_rcp.n)]
                            if ch >= 2:
                                dvw.append((c_rs, rs_done_chunk[ch - 2]))
                            for j in range(8):
                                h, i = divmod(j, 4)
                                last = (j == 7)
                                p.emit("vector",
                                       lambda e, jj=j, ii=i, h2=h, pr2=pr, cp=cpar: e.tensor_scalar(
                                           o_chunk[:, cp, ii, pr2 * 128 + h2 * 64:pr2 * 128 + (h2 + 1) * 64],
                                           ttm[:, jj, 0:64],
                                           recb[:, jj:jj + 1], None,
                                           op0=ALU.mult),
                                       waits=dvw if j == 0 else [],
                                       inc=c_div if last else None)
                            fin["div"] = c_div.n
                        return fin_emit
                    pending_fin.append(make_finalize())
                    pump(2)
                while pending_fin:
                    pending_fin.pop(0)()
                bg.append(residual_gen(ch, c_div.n))
            while bg:
                pump(1)

            attn_done = [(c_sc, c_sc.n), (c_av, c_av.n),
                         (c_exp, c_exp.n), (c_div, c_div.n),
                         (c_rcp, c_rcp.n), (c_oev, c_oev.n),
                         (c_otr, c_otr.n), (c_lno, c_lno.n), (c_res, c_res.n),
                         (c_rs, c_rs.n), (c_rv, c_rv.n), (c_rr, c_rr.n),
                         (c_xp, c_xp.n),
                         (d_xp, d_xp.n), (d_h2, d_h2.n), (d_x2, d_x2.n),
                         (d_kv, d_kv.n), (d_qc, d_qc.n)]

        es3 = ExitStack()
        S3 = lambda n, sh, dt: es3.enter_context(nc.sbuf_tensor(n, sh, dt))
        P3 = lambda n, sh, dt: es3.enter_context(nc.psum_tensor(n, sh, dt))
        with es3:
            h2T = S3("h2T", [128, 8, QT], BF16)
            faq = S3("faq", [128, 8, QT], F32)
            zb = S3("zb", [128, 16, QT], BF16)
            zs = S3("zs", [128, QT], F32)
            w2d = S3("w2d", [128, 2, 16, 128], BF16)
            cwb = S3("cwb", [128, QT], F32)
            tmpf = S3("tmpf", [128, 512], F32)
            ctsb = S3("ctsb", [8, 128], F32)
            cwrow = S3("cwrow", [1, QT], F32)
            lgT = S3("lgT", [8, QT], F32)
            lgtm = S3("lgtm", [128, 8], F32)
            m1b = S3("m1b", [128, 4], F32)
            mskb = S3("mskb", [128, 8], F32)
            combtm = S3("combtm", [128, 8], F32)
            ftile = S3("ftile", [128, 2, D], F32)
            sqb3 = S3("sqb3", [128, 2, D], F32)
            stb3 = S3("stb3", [128, 2, 8], F32)
            lnf = S3("lnf", [128, 2, D], F32)
            xpl = S3("xpl", [128, 2, D], F32)
            outb = S3("outb", [128, 2, D], F32)
            z1_ps = P3("z1_ps", [128, 2, 512], F32)
            z3_ps = P3("z3_ps", [128, 2, 512], F32)
            y2_ps = P3("y2_ps", [128, 2, 512], F32)
            mi_ps = P3("mi_ps", [128, 2, 512], F32)



            psum_done = [(c_exp, c_exp.n), (c_oev, c_oev.n), (c_div, c_div.n)]

            w13_done = {}
            w2_done = {}
            z1_snap = [0, 0]
            z3_snap = [0, 0]
            y2_snap = [0, 0]
            z_reads = 0
            h2T_reads = 0
            fu_all = 0
            fsnap = {"ftr": 0, "ftc": 0}
            fv_snap = [0, 0]
            cw_read_snap = 0

            def emit_w13_load(q, e_i, c, extra=()):
                gi = q * NE + e_i
                need = gi * 16 + 4 * c - 4
                w = list(extra) + ([(c_z3, need)] if need > 0 else [])
                p.emit("gpsimd",
                       lambda en, ei=e_i, cc=c, pr=c % 2: en.dma_start(
                           w1c[:, pr, :, :], wa1t_d[ei, cc]),
                       waits=w, inc=d_w13, dma=True)
                p.emit("gpsimd",
                       lambda en, ei=e_i, cc=c, pr=c % 2: en.dma_start(
                           w3c[:, pr, :, :], wa3t_d[ei, cc]),
                       inc=d_w13, dma=True)
                w13_done[(q, e_i, c)] = d_w13.n

            def emit_w2_load(q, e_i, dt, extra=()):
                gi = q * NE + e_i
                need = gi * 16 + 2 * dt - 2
                w = list(extra) + ([(c_y2, need)] if need > 0 else [])
                if gi == 0 and dt < 2:
                    w = w + list(attn_done)
                p.emit("gpsimd",
                       lambda en, ei=e_i, d2=dt: en.dma_start(
                           w2d[:, d2 % 2, :, :], wa2t_d[ei, d2]),
                       waits=w, inc=d_w2, dma=True)
                w2_done[(q, e_i, dt)] = d_w2.n

            wload_seq = []
            for q in range(NQ):
                for e_i in range(NE):
                    for c in range(4):
                        wload_seq.append(("w13", q, e_i, c))
                    for dt in range(8):
                        wload_seq.append(("w2", q, e_i, dt))
            wl_idx = [0]

            def pump_wloads(n, extra=()):
                for _ in range(n):
                    if wl_idx[0] >= len(wload_seq):
                        return
                    item = wload_seq[wl_idx[0]]
                    wl_idx[0] += 1
                    if item[0] == "w2":
                        emit_w2_load(item[1], item[2], item[3], extra=extra)
                    else:
                        emit_w13_load(item[1], item[2], item[3], extra=extra)

            h2T_loaded = [None] * NQ
            h2T_read_snap = [0, 0, 0, 0]

            def h2T_tile(qq):
                return h2T_pre if qq % 2 == 0 else h2T

            h2T_loaded[0] = (d_h2p, d_h2p.n)

            def emit_h2T_load(qq):
                w = [(d_h2, 16 * 4 * (2 * qq + 2))]
                if qq == 1:
                    w += list(attn_done)
                elif qq >= 2:
                    w += [(c_z1, h2T_read_snap[qq - 2]), (c_lg, 2 * (qq - 1))]
                eng = "sync" if qq % 2 == 0 else "scalar"
                ch2 = d_h2p if qq % 2 == 0 else d_h2T
                dst = h2T_tile(qq)
                for dt in range(8):
                    p.emit(eng,
                           lambda e, d=dt, q2=qq, t=dst: e.dma_start_transpose(
                               t[:, d, :],
                               h2_scr[q2 * QT:(q2 + 1) * QT, d * 128:(d + 1) * 128]),
                           waits=w, inc=ch2, dma=True)
                h2T_loaded[qq] = (ch2, ch2.n)

            wl_idx[0] = 2
            w13_done[(0, 0, 0)] = 32
            w13_done[(0, 0, 1)] = 64
            pump_wloads(1)
            emit_h2T_load(1)
            pending_ffn = []
            for q in range(NQ):
                p.scope(f"ffq{q}")

                def lg_mms(e, half, qq):
                    ins = None
                    t = h2T_tile(qq)
                    for dt in range(8):
                        ins = e.matmul(mi_ps[0:8, half, :], rwt_sb[:, dt, :],
                                       t[:, dt, half * 512:(half + 1) * 512],
                                       start=(dt == 0), stop=(dt == 7))
                    return ins
                for half in range(2):
                    lgw = [(d_rw, 16), h2T_loaded[q],
                           (c_lgev, c_lgev.n), (c_ftc, fsnap["ftc"]), (c_cw, c_cw.n)]
                    if q == 0 and half == 0:
                        lgw += [(c_rcp, c_rcp.n), (c_div, c_div.n),
                                (c_av, c_av.n), (c_exp, c_exp.n),
                                (c_oev, c_oev.n), (c_otr, c_otr.n)]
                    p.emit("tensor",
                           lambda e, h=half, qq=q: lg_mms(e, h, qq),
                           waits=lgw, inc=c_lg)
                p.emit("vector",
                       lambda e: e.tensor_scalar(
                           lgT[:], mi_ps[0:8, :, :].rearrange("p a b -> p (a b)"),
                           ebias[:, 0:1], None, op0=ALU.add),
                       waits=[(c_lg, c_lg.n), (d_c, 64), (c_ltr, c_ltr.n)],
                       inc=c_lgev)
                lgev_now = c_lgev.n

                def make_tk(i, q, lgev_now=None):
                    def tk_emit():
                        p.emit("tensor",
                               lambda e, ii=i: e.transpose(
                                   mi_ps[:, 0, 0:8], lgT[0:8, ii * 128:(ii + 1) * 128],
                                   id32[0:8, 0:8]),
                               waits=[(c_lgev, lgev_now), (c_tk, c_tk.n),
                                      (c_ftc, c_ftc.n)],
                               inc=c_ltr)
                        p.emit("vector",
                               lambda e: e.tensor_copy(lgtm[:], mi_ps[:, 0, 0:8]),
                               waits=[(c_ltr, c_ltr.n)], inc=c_tk)
                        p.emit("vector", lambda e: e.reduce_max(m1b[:, 0:1], lgtm[:], axis=AX.X))
                        p.emit("vector",
                               lambda e: e.tensor_scalar(mskb[:], lgtm[:], m1b[:, 0:1], -1e30,
                                                         op0=ALU.is_ge, op1=ALU.mult))
                        p.emit("vector",
                               lambda e: e.tensor_tensor(mskb[:], lgtm[:], mskb[:], op=ALU.add))
                        p.emit("vector", lambda e: e.reduce_max(m1b[:, 1:2], mskb[:], axis=AX.X))
                        p.emit("vector",
                               lambda e: e.tensor_scalar(m1b[:, 2:3], m1b[:, 1:2], m1b[:, 0:1], None,
                                                         op0=ALU.subtract),
                               inc=c_tk)
                        p.emit("scalar",
                               lambda e: e.activation(m1b[:, 2:3], m1b[:, 2:3], AF.Exp),
                               waits=[(c_tk, c_tk.n)], inc=c_rr)
                        p.emit("vector",
                               lambda e: e.tensor_scalar(m1b[:, 2:3], m1b[:, 2:3], 1.0, None, op0=ALU.add),
                               waits=[(c_rr, c_rr.n)])
                        p.emit("vector", lambda e: e.reciprocal(m1b[:, 3:4], m1b[:, 2:3]))
                        p.emit("vector",
                               lambda e: e.tensor_scalar(combtm[:], lgtm[:], m1b[:, 0:1], None,
                                                         op0=ALU.subtract),
                               inc=c_tk)
                        p.emit("scalar",
                               lambda e: e.activation(combtm[:], combtm[:], AF.Exp),
                               waits=[(c_tk, c_tk.n)], inc=c_rr)
                        p.emit("vector",
                               lambda e: e.tensor_scalar(mskb[:], lgtm[:], m1b[:, 1:2], None,
                                                         op0=ALU.is_ge),
                               waits=[(c_rr, c_rr.n)])
                        p.emit("vector",
                               lambda e: e.tensor_tensor(combtm[:], combtm[:], mskb[:], op=ALU.mult))
                        p.emit("vector",
                               lambda e: e.tensor_scalar(combtm[:], combtm[:], m1b[:, 3:4], None,
                                                         op0=ALU.mult),
                               inc=c_tk)
                        p.emit("tensor",
                               lambda e: e.transpose(mi_ps[0:8, 1, 0:128], combtm[:], id32[:]),
                               waits=[(c_tk, c_tk.n), (c_cmb, c_cmb.n)], inc=c_ctr)
                        p.emit("vector",
                               lambda e: e.tensor_copy(ctsb[:], mi_ps[0:8, 1, 0:128]),
                               waits=[(c_ctr, c_ctr.n), (d_cws, d_cws.n)], inc=c_cmb)
                        p.emit("sync",
                               lambda e, t=q * 8 + i: e.dma_start(
                                   cwT_scr[:, t * 128:(t + 1) * 128], ctsb[:]),
                               waits=[(c_cmb, c_cmb.n)], inc=d_cws, dma=True)
                    return tk_emit
                for i in range(8):
                    pending_ffn.append(make_tk(i, q, lgev_now))

                for e_i in range(NE):
                    routed = e_i >= 2
                    if routed:
                        while pending_ffn:
                            pending_ffn.pop(0)()
                        p.emit("sync",
                               lambda e, ei=e_i, qq=q: e.dma_start(
                                   cwrow[:], cwT_scr[ei - 2:ei - 1, qq * QT:(qq + 1) * QT]),
                               waits=[(d_cws, 16 * 8 * (q + 1)), (c_cwm, c_cwm.n)],
                               inc=d_cwl, dma=True)
                        for half in range(2):
                            p.emit("tensor",
                                   lambda e, h=half: e.matmul(
                                       mi_ps[:, h, :], ones32[0:1, :],
                                       cwrow[0:1, h * 512:(h + 1) * 512],
                                       start=True, stop=True),
                                   waits=[(d_cwl, d_cwl.n), (c_ftc, fsnap["ftc"]),
                                          (c_ltr, c_ltr.n), (c_ctr, c_ctr.n),
                                          (c_cw, c_cw.n)],
                                   inc=c_cwm)
                        p.emit("vector",
                               lambda e: e.tensor_copy(
                                   cwb[:], mi_ps[:, :, :].rearrange("p a b -> p (a b)")),
                               waits=[(c_cwm, c_cwm.n), (c_fu, cw_read_snap)],
                               inc=c_cw)

                    for ht in range(16):
                        c = ht // 4
                        pump_wloads(1)
                        if pending_ffn and (e_i > 0 or ht >= 4):
                            pending_ffn.pop(0)()

                        def z_mms(e, w, ps, h=ht, cc=c, qq=q):
                            ins = None
                            t = h2T_tile(qq)
                            for half in range(2):
                                for dt in range(8):
                                    ins = e.matmul(
                                        ps[:, half, :],
                                        w[:, cc % 2, dt, (h % 4) * 128:(h % 4 + 1) * 128],
                                        t[:, dt, half * 512:(half + 1) * 512],
                                        start=(dt == 0), stop=(dt == 7),
                                        skip_group_check=True)
                            return ins
                        z1w = [(d_w13, w13_done[(q, e_i, c)]),
                               h2T_loaded[q],
                               (c_si, z1_snap[0])]
                        if q == 0 and e_i == 0 and ht == 0:
                            z1w += psum_done
                        p.emit("tensor",
                               lambda e, h=ht, cc=c, qq=q: z_mms(e, w1c, z1_ps, h, cc, qq),
                               waits=z1w,
                               inc=c_z1)
                        z1_snap[0] = c_si.n + 1
                        h2T_reads = c_z1.n
                        z3w = [(c_zm, z3_snap[0])]
                        if q == 0 and e_i == 0 and ht == 0:
                            z3w += psum_done
                        p.emit("tensor",
                               lambda e, h=ht, cc=c, qq=q: z_mms(e, w3c, z3_ps, h, cc, qq),
                               waits=z3w,
                               inc=c_z3)
                        z3_snap[0] = c_zm.n + 1
                        p.emit("scalar",
                               lambda e: e.activation(
                                   zs[:], z1_ps[:, :, :].rearrange("p a b -> p (a b)"), AF.Silu),
                               waits=[(c_z1, c_z1.n), (c_zm, c_zm.n)], inc=c_si)
                        p.emit("vector",
                               lambda e, h=ht: e.tensor_tensor(
                                   zb[:, h, :], zs[:],
                                   z3_ps[:, :, :].rearrange("p a b -> p (a b)"), op=ALU.mult),
                               waits=[(c_si, c_si.n), (c_z3, c_z3.n), (c_y2, z_reads)],
                               inc=c_zm)
                    zm_all = c_zm.n

                    for dt in range(8):
                        for half in range(2):
                            par = (dt * 2 + half) % 2

                            def y2_mms(e, d=dt, h=half, pr=par):
                                ins = None
                                for ht in range(16):
                                    ins = e.matmul(
                                        y2_ps[:, pr, :],
                                        w2d[:, d % 2, ht, :],
                                        zb[:, ht, h * 512:(h + 1) * 512],
                                        start=(ht == 0), stop=(ht == 15),
                                        skip_group_check=True)
                                return ins
                            y2w = [(c_zm, zm_all), (d_w2, w2_done[(q, e_i, dt)]),
                                   (c_fu, y2_snap[par])]
                            if q == 0 and e_i == 0 and dt == 0:
                                y2w += psum_done
                            p.emit("tensor", y2_mms, waits=y2w, inc=c_y2)
                            if e_i == 0:
                                p.emit("vector",
                                       lambda e, d=dt, h=half, pr=par: e.tensor_copy(
                                           faq[:, d, h * 512:(h + 1) * 512], y2_ps[:, pr, :]),
                                       waits=[(c_y2, c_y2.n), (c_ftr, fsnap["ftr"])],
                                       inc=c_fu)
                            elif not routed:
                                p.emit("vector",
                                       lambda e, d=dt, h=half, pr=par: e.scalar_tensor_tensor(
                                           faq[:, d, h * 512:(h + 1) * 512], y2_ps[:, pr, :],
                                           1.0, faq[:, d, h * 512:(h + 1) * 512],
                                           op0=ALU.mult, op1=ALU.add),
                                       waits=[(c_y2, c_y2.n)], inc=c_fu)
                            else:
                                p.emit("vector",
                                       lambda e, h=half, pr=par: e.tensor_tensor(
                                           tmpf[:], y2_ps[:, pr, :],
                                           cwb[:, h * 512:(h + 1) * 512], op=ALU.mult),
                                       waits=[(c_y2, c_y2.n), (c_cw, c_cw.n)])
                                p.emit("vector",
                                       lambda e, d=dt, h=half: e.tensor_tensor(
                                           faq[:, d, h * 512:(h + 1) * 512],
                                           faq[:, d, h * 512:(h + 1) * 512], tmpf[:], op=ALU.add),
                                       inc=c_fu)
                            y2_snap[par] = c_fu.n
                    z_reads = c_y2.n
                    w2_reads_snap = c_y2.n
                    cw_read_snap = c_fu.n
                fu_all = c_fu.n
                h2T_read_snap[q] = c_z1.n
                if q + 2 < NQ:
                    emit_h2T_load(q + 2)

                p.scope(f"fin{q}")
                oadd_done = [c_oadd.n, c_oadd.n]

                def make_qfin(i, q=q, fu_all_q=None, oadd_done=oadd_done):
                    tt = q * 8 + i
                    par = i % 2

                    def qfin_emit():
                        def mi8(dt):
                            return mi_ps[:, dt // 4, (dt % 4) * 128:(dt % 4 + 1) * 128]

                        def ftr_mms(e, ii=i, mi8=mi8):
                            ins = None
                            for dt in range(8):
                                ins = e.transpose(mi8(dt),
                                                  faq[:, dt, ii * 128:(ii + 1) * 128], id32[:])
                            return ins
                        p.emit("tensor", ftr_mms,
                               waits=[(c_fu, fu_all_q), (c_ftc, c_ftc.n),
                                      (c_cw, c_cw.n), (c_ltr, c_ltr.n),
                                      (c_lgev, c_lgev.n), (c_ctr, c_ctr.n),
                                      (c_cwm, c_cwm.n)],
                               inc=c_ftr)
                        fsnap["ftr"] = c_ftr.n
                        p.emit("vector",
                               lambda e, pr=par: e.tensor_copy(
                                   ftile[:, pr, :], mi_ps[:, :, :].rearrange("p a b -> p (a b)")),
                               waits=[(c_ftr, c_ftr.n)], inc=c_ftc)
                        fsnap["ftc"] = c_ftc.n
                        emit_ln(ftile[:, par, :], lnf[:, par, :], sqb3[:, par, :], stb3[:, par, :],
                                c_fs, c_fv, c_fr, c_fev,
                                waits_src_act=[(c_ftc, c_ftc.n), (c_fv, fv_snap[par])],
                                waits_src_dve=[(c_ftc, c_ftc.n)])
                        fv_snap[par] = c_fv.n
                        p.emit("sync",
                               lambda e, t=tt, pr=par: e.dma_start(
                                   xpl[:, pr, :], xp_scr[t * 128:(t + 1) * 128, :]),
                               waits=[(d_xp, 16 * (tt + 1)), (c_oadd, oadd_done[par])],
                               inc=d_xpl, dma=True)
                        p.emit("vector",
                               lambda e, pr=par: e.tensor_tensor(
                                   outb[:, pr, :], xpl[:, pr, :], lnf[:, pr, :], op=ALU.add),
                               waits=[(d_xpl, d_xpl.n), (d_out, 16 * max(0, tt - 1))],
                               inc=c_oadd)
                        oadd_done[par] = c_oadd.n
                        p.emit("sync",
                               lambda e, t=tt, pr=par: e.dma_start(
                                   out_d[t * 128:(t + 1) * 128, :], outb[:, pr, :]),
                               waits=[(c_oadd, c_oadd.n)], inc=d_out, dma=True)
                    return qfin_emit

                for i in range(8):
                    pending_ffn.append(make_qfin(i, fu_all_q=fu_all))

            while pending_ffn:
                pending_ffn.pop(0)()
            p.emit("sync", lambda e: e.nop(), waits=[(d_out, 16 * NT)])

        p.simulate()

        block = es0.enter_context(nc.Block())
        p.build2(block)

    return nc


_CACHE = {}


def _get_nc():
    if "nc" not in _CACHE:
        _CACHE["nc"] = build()
    return _CACHE["nc"]


def kernel(L0_attn_inputs, L0_attn_Wq, L0_attn_Wk, L0_attn_Wv,
           L0_ffn_router_w, L0_ffn_expert_bias, L0_ffn_w1, L0_ffn_w3,
           L0_ffn_w2, L0_ffn_sw1, L0_ffn_sw3, L0_ffn_sw2):
    nc = _get_nc()
    f32 = np.float32
    ac = np.ascontiguousarray
    x = np.asarray(L0_attn_inputs, dtype=f32)
    def _wqk(w):
        return ac(np.asarray(w, f32).reshape(8, 128, 8, 128).transpose(2, 1, 0, 3))
    wq, wk = _wqk(L0_attn_Wq), _wqk(L0_attn_Wk)
    wv = ac(np.asarray(L0_attn_Wv, f32).reshape(8, 128, 2, 512).transpose(2, 1, 0, 3))
    rwt = ac(np.asarray(L0_ffn_router_w, f32).T.reshape(8, 128, 8).transpose(1, 0, 2))
    eb = ac(np.asarray(L0_ffn_expert_bias, f32).reshape(8, 1))
    def _w13(sw, w):
        a = np.concatenate([np.asarray(sw, f32), np.asarray(w, f32)], 0).transpose(0, 2, 1)
        return ac(a.reshape(NE, 8, 128, 4, 512).transpose(0, 3, 2, 1, 4))
    wa1t = _w13(L0_ffn_sw1, L0_ffn_w1)
    wa3t = _w13(L0_ffn_sw3, L0_ffn_w3)
    a2 = np.concatenate([np.asarray(L0_ffn_sw2, f32), np.asarray(L0_ffn_w2, f32)], 0) \
           .transpose(0, 2, 1)
    wa2t = ac(a2.reshape(NE, 16, 128, 8, 128).transpose(0, 3, 2, 1, 4))
    id32 = np.eye(128, dtype=f32)
    onesbf = np.ones((128, 1), dtype=ml_dtypes.bfloat16)
    ones32 = np.ones((1, 128), dtype=f32)

    in_maps = []
    for c in range(8):
        in_maps.append({
            "x": ac(x[c]), "wq": wq, "wk": wk, "wv": wv,
            "rwt": rwt, "eb": eb, "wa1t": wa1t, "wa3t": wa3t, "wa2t": wa2t,
            "id32": id32, "onesbf": onesbf, "ones32": ones32,
        })
    res = run_bass_kernel_spmd(nc, in_maps, core_ids=list(range(8)))
    return np.stack([r["out"] for r in res.results], axis=0)


# version 16 (final)
# speedup vs baseline: 1.1683x; 1.0171x over previous
import numpy as np
import ml_dtypes
from contextlib import ExitStack

import concourse.bass as bass
import concourse.mybir as mybir
from concourse.bass_utils import run_bass_kernel_spmd

F32 = mybir.dt.float32
BF16 = mybir.dt.bfloat16
AF = mybir.ActivationFunctionType
ALU = mybir.AluOpType
AX = mybir.AxisListType

S = 4096
D = 1024
NT = S // 128
NCH = 8
KT = 32
NG = 4
NP = 8
HID = 2048
NE = 10
NQ = 4
QT = S // NQ
EPS = 1e-5

ENGINES = ("sync", "gpsimd", "tensor", "vector", "scalar")


class Chan:
    __slots__ = ("sem", "name", "n")

    def __init__(self, sem, name):
        self.sem = sem
        self.name = name
        self.n = 0


AUTOCHAIN = ("vector", "scalar")


class Prog:
    def __init__(self, nc, es):
        self.nc = nc
        self.es = es
        self.ops = {e: [] for e in ENGINES}
        self.cseq = {}
        self.last_marker = {}

    def init_autochain(self):
        for e in AUTOCHAIN:
            self.cseq[e] = self.chan(f"cseq_{e}")

    def chan(self, name):
        sem = self.es.enter_context(self.nc.semaphore(name))
        return Chan(sem, name)

    def scope(self, name):
        self.cur_scope = name

    def emit(self, eng, fn, waits=(), inc=None, dma=False, nochain=False):
        w = [(c, v) for (c, v) in waits if v > 0]
        if eng in self.cseq and not dma:
            lm = self.last_marker.get(eng)
            if lm is not None and not nochain:
                w.append(lm)
            if inc is None:
                inc = self.cseq[eng]
        self.ops[eng].append((w, fn, inc, dma, getattr(self, "cur_scope", "x")))
        if inc is not None:
            inc.n += 16 if dma else 1
            if eng in self.cseq and not dma:
                self.last_marker[eng] = (inc, inc.n)
        return inc.n if inc is not None else None

    def build(self, block):
        def mk(name):
            def body(eng):
                hwm = {}
                for w, fn, inc, dma in self.ops[name]:
                    for c, v in w:
                        if hwm.get(c.name, 0) < v:
                            eng.wait_ge(c.sem, v)
                            hwm[c.name] = v
                ins = fn(eng)
                if inc is not None:
                    ins.then_inc(inc.sem, 16 if dma else 1)
            return body

        raise RuntimeError("use build2")

    def build2(self, block):
        prog = self
        nc = self.nc

        def run_stream(name, eng):
            hwm = {}
            for w, fn, inc, dma, sc in prog.ops[name]:
                with nc.named_scope(sc):
                    for c, v in w:
                        if hwm.get(c.name, 0) < v:
                            eng.wait_ge(c.sem, v)
                            hwm[c.name] = v
                    ins = fn(eng)
                    if inc is not None:
                        ins.then_inc(inc.sem, 16 if dma else 1)

        @block.sync
        def _(e):
            run_stream("sync", e)

        @block.gpsimd
        def _(e):
            run_stream("gpsimd", e)

        @block.tensor
        def _(e):
            run_stream("tensor", e)

        @block.vector
        def _(e):
            run_stream("vector", e)

        @block.scalar
        def _(e):
            run_stream("scalar", e)

    def simulate(self):
        counts = {}
        idx = {e: 0 for e in ENGINES}
        total = sum(len(v) for v in self.ops.values())
        done = 0
        while done < total:
            progress = False
            for e in ENGINES:
                while idx[e] < len(self.ops[e]):
                    w, fn, inc, dma, _sc = self.ops[e][idx[e]]
                    if all(counts.get(c.name, 0) >= v for c, v in w):
                        if inc is not None:
                            counts[inc.name] = counts.get(inc.name, 0) + (16 if dma else 1)
                        idx[e] += 1
                        done += 1
                        progress = True
                    else:
                        break
            if not progress:
                blocked = {}
                for e in ENGINES:
                    if idx[e] < len(self.ops[e]):
                        w, _, _, _, _sc = self.ops[e][idx[e]]
                        blocked[e] = [(c.name, v, counts.get(c.name, 0)) for c, v in w
                                      if counts.get(c.name, 0) < v]
                raise RuntimeError(f"deadlock at op idx {idx}; blocked: {blocked}")


def build(dbg=False):
    nc = bass.Bass("TRN2", target_bir_lowering=False)

    x_d = nc.dram_tensor("x", [S, D], F32, kind="ExternalInput").ap()
    wq_d = nc.dram_tensor("wq", [8, 128, 8, 128], F32, kind="ExternalInput").ap()
    wk_d = nc.dram_tensor("wk", [8, 128, 8, 128], F32, kind="ExternalInput").ap()
    wv_d = nc.dram_tensor("wv", [2, 128, 8, 512], F32, kind="ExternalInput").ap()
    rwt_d = nc.dram_tensor("rwt", [128, 8, 8], F32, kind="ExternalInput").ap()
    eb_d = nc.dram_tensor("eb", [8, 1], F32, kind="ExternalInput").ap()
    wa1t_d = nc.dram_tensor("wa1t", [NE, 4, 128, 8, 512], F32, kind="ExternalInput").ap()
    wa3t_d = nc.dram_tensor("wa3t", [NE, 4, 128, 8, 512], F32, kind="ExternalInput").ap()
    wa2t_d = nc.dram_tensor("wa2t", [NE, 8, 128, 16, 128], F32, kind="ExternalInput").ap()
    id32_d = nc.dram_tensor("id32", [128, 128], F32, kind="ExternalInput").ap()
    onesbf_d = nc.dram_tensor("onesbf", [128, 1], BF16, kind="ExternalInput").ap()
    ones32_d = nc.dram_tensor("ones32", [1, 128], F32, kind="ExternalInput").ap()
    out_d = nc.dram_tensor("out", [S, D], F32, kind="ExternalOutput").ap()

    skind = "ExternalOutput" if dbg else "Internal"
    h_scr = nc.dram_tensor("h_scr", [S, D], BF16, kind=skind).ap()
    qT_scr = nc.dram_tensor("qT_scr", [NCH, 128, NP, 512], BF16, kind=skind).ap()
    kT_scr = nc.dram_tensor("kT_scr", [NP, 128, S], BF16, kind=skind).ap()
    v_scr = nc.dram_tensor("v_scr", [NP, 128, NT, 2, 64], BF16, kind=skind).ap()
    xp_scr = nc.dram_tensor("xp_scr", [S, D], F32, kind=skind).ap()
    h2_scr = nc.dram_tensor("h2_scr", [S, D], BF16, kind=skind).ap()
    cwT_scr = nc.dram_tensor("cwT_scr", [8, S], F32, kind=skind).ap()
    wa1b_d = nc.dram_tensor("wa1b", [NE, 4, 128, 8, 512], BF16, kind="Internal").ap()
    wa3b_d = nc.dram_tensor("wa3b", [NE, 4, 128, 8, 512], BF16, kind="Internal").ap()
    wa2b_d = nc.dram_tensor("wa2b", [NE, 8, 128, 16, 128], BF16, kind="Internal").ap()

    es0 = ExitStack()
    with es0:
        p = Prog(nc, es0)
        p.init_autochain()
        SB = lambda n, sh, dt: es0.enter_context(nc.sbuf_tensor(n, sh, dt))

        id32 = SB("id32s", [128, 128], F32)
        onesbf = SB("onesbfs", [128, 1], BF16)
        ones32 = SB("ones32s", [1, 128], F32)
        biasg = SB("biasg", [128, 8], F32)
        ebias = SB("ebiass", [8, 1], F32)
        epsb = SB("epsb", [128, 1], F32)
        rwt_sb = SB("rwt_sb", [128, 8, 8], BF16)

        d_c = p.chan("d_c")
        d_x = p.chan("d_x")
        d_h = p.chan("d_h")
        d_hT = p.chan("d_hT")
        d_wqkv = p.chan("d_wqkv")
        d_qk = p.chan("d_qk")
        d_v = p.chan("d_v")
        d_qc = p.chan("d_qc")
        d_h2p = p.chan("d_h2p")
        d_kv = p.chan("d_kv")
        d_x2 = p.chan("d_x2")
        d_xp = p.chan("d_xp")
        d_h2 = p.chan("d_h2")
        d_h2T = p.chan("d_h2T")
        d_w13 = p.chan("d_w13")
        d_w2 = p.chan("d_w2")
        d_cbk = p.chan("d_cbk")
        d_rw = p.chan("d_rw")
        d_cws = p.chan("d_cws")
        d_cwl = p.chan("d_cwl")
        d_xpl = p.chan("d_xpl")
        d_out = p.chan("d_out")
        c_sq = p.chan("c_sq")
        c_var = p.chan("c_var")
        c_rstd = p.chan("c_rstd")
        c_ln = p.chan("c_ln")
        c_qk = p.chan("c_qk")
        c_qkev = p.chan("c_qkev")
        c_sq2 = p.chan("c_sq2")
        c_nrm = p.chan("c_nrm")
        c_nred = p.chan("c_nred")
        c_mbv = p.chan("c_mbv")
        c_mbb = p.chan("c_mbb")
        c_mb = p.chan("c_mb")
        c_v = p.chan("c_v")
        c_vev = p.chan("c_vev")
        c_sc = p.chan("c_sc")
        c_exp = p.chan("c_exp")
        c_av = p.chan("c_av")
        c_rcp = p.chan("c_rcp")
        c_oev = p.chan("c_oev")
        c_otr = p.chan("c_otr")
        c_div = p.chan("c_div")
        c_lno = p.chan("c_lno")
        c_xp = p.chan("c_xp")
        c_res = p.chan("c_res")
        c_rv = p.chan("c_rv")
        c_rr = p.chan("c_rr")
        c_rs = p.chan("c_rs")
        c_lg = p.chan("c_lg")
        c_lgev = p.chan("c_lgev")
        c_ltr = p.chan("c_ltr")
        c_tk = p.chan("c_tk")
        c_ctr = p.chan("c_ctr")
        c_cmb = p.chan("c_cmb")
        c_cw = p.chan("c_cw")
        c_cwm = p.chan("c_cwm")
        c_z1 = p.chan("c_z1")
        c_z3 = p.chan("c_z3")
        c_si = p.chan("c_si")
        c_zm = p.chan("c_zm")
        c_y2 = p.chan("c_y2")
        c_fu = p.chan("c_fu")
        c_ftr = p.chan("c_ftr")
        c_fev = p.chan("c_fev")
        c_ftc = p.chan("c_ftc")
        c_fv = p.chan("c_fv")
        c_fr = p.chan("c_fr")
        c_fs = p.chan("c_fs")
        c_oadd = p.chan("c_oadd")

        c_cst = p.chan("c_cst")
        h2T_pre = SB("h2T_pre", [128, 8, QT], BF16)
        w1c = SB("w1c", [128, 2, 8, 512], BF16)
        w3c = SB("w3c", [128, 2, 8, 512], BF16)
        p.scope("consts")
        p.emit("vector", lambda e: e.memset(epsb[:], EPS), inc=c_cst)
        p.emit("sync", lambda e, a=id32, b=id32_d: e.dma_start(a[:], b[:]), inc=d_c, dma=True)
        p.emit("sync", lambda e, a=onesbf, b=onesbf_d: e.dma_start(a[:], b[:]), inc=d_c, dma=True)
        p.emit("sync", lambda e, a=ones32, b=ones32_d: e.dma_start(a[:], b[:]), inc=d_c, dma=True)
        p.emit("sync", lambda e, a=ebias, b=eb_d: e.dma_start(a[:], b[:]), inc=d_c, dma=True)
        p.emit("gpsimd", lambda e: e.dma_start(rwt_sb[:], rwt_d[:]), inc=d_rw, dma=True)

        def emit_ln(src, dst, sq, st, ch_sq, ch_var, ch_rstd, ch_out,
                    waits_src_act=(), waits_src_dve=(), extra_final_waits=()):
            p.emit("vector",
                   lambda e, s=src, q=sq: e.tensor_tensor(q, s, s, op=ALU.mult),
                   waits=list(waits_src_act) + list(waits_src_dve), inc=ch_sq)
            p.emit("vector",
                   lambda e, s=src, t=st: e.reduce_sum(t[:, 0:1], s, axis=AX.X),
                   waits=waits_src_dve)
            p.emit("vector",
                   lambda e, q=sq, t=st: e.reduce_sum(t[:, 1:2], q, axis=AX.X),
                   waits=[(ch_sq, ch_sq.n)])
            p.emit("vector",
                   lambda e, t=st: e.tensor_scalar(t[:, 2:3], t[:, 0:1], 1.0 / D, None, op0=ALU.mult))
            p.emit("vector",
                   lambda e, t=st: e.tensor_scalar(t[:, 3:4], t[:, 1:2], 1.0 / D, None, op0=ALU.mult))
            p.emit("vector",
                   lambda e, t=st: e.tensor_tensor(t[:, 4:5], t[:, 2:3], t[:, 2:3], op=ALU.mult))
            p.emit("vector",
                   lambda e, t=st: e.tensor_tensor(t[:, 5:6], t[:, 3:4], t[:, 4:5], op=ALU.subtract),
                   inc=ch_var)
            p.emit("scalar",
                   lambda e, t=st: e.activation(t[:, 6:7], t[:, 5:6], AF.Sqrt,
                                                bias=epsb[0:t.shape[0], 0:1]),
                   waits=[(ch_var, ch_var.n), (c_cst, 1)], inc=ch_rstd)
            p.emit("vector",
                   lambda e, t=st: e.reciprocal(t[:, 7:8], t[:, 6:7]),
                   waits=[(ch_rstd, ch_rstd.n)], inc=ch_rstd)
            p.emit("vector",
                   lambda e, s=src, d=dst, t=st: e.tensor_scalar(
                       d, s, t[:, 2:3], t[:, 7:8], op0=ALU.subtract, op1=ALU.mult),
                   waits=[(ch_rstd, ch_rstd.n)] + list(extra_final_waits), inc=ch_out)

        es1 = ExitStack()
        S1 = lambda n, sh, dt: es1.enter_context(nc.sbuf_tensor(n, sh, dt))
        P1 = lambda n, sh, dt: es1.enter_context(nc.psum_tensor(n, sh, dt))
        with es1:
            xb = S1("xb", [128, 2, D], F32)
            sqb = S1("sqb", [128, 2, D], F32)
            stb = S1("stb", [128, 2, 8], F32)
            hb = S1("hb", [128, 2, D], BF16)
            hT = S1("hT", [128, 8, S], BF16)
            wq_sb = S1("wq_sb", [128, 8, 8, 128], BF16)
            wk_sb = S1("wk_sb", [128, 8, 8, 128], BF16)
            wv_sb = S1("wv_sb", [128, 2, 8, 512], BF16)
            ev_qk = S1("ev_qk", [128, 2, 512], BF16)
            sq2 = S1("sq2", [128, 2, 512], BF16)
            nmax = S1("nmax", [1, 256], F32)
            mbh = S1("mbh", [1, 64], F32)
            ev_v = S1("ev_v", [128, 2, 512], BF16)
            pq = P1("pq", [128, 4, 512], F32)
            pnrm = P1("pnrm", [128, 2, 512], F32)
            pbc = P1("pbc", [128, 512], F32)

            for pi2 in range(8):
                for w_sb, w_d in ((wq_sb, wq_d), (wk_sb, wk_d)):
                    p.emit("gpsimd",
                           lambda e, a=w_sb, b=w_d, i=pi2: e.dma_start(
                               a[:, i, :, :], b[i]),
                           inc=d_wqkv, dma=True)
            for hh in range(2):
                p.emit("gpsimd",
                       lambda e, h=hh: e.dma_start(wv_sb[:, h, :, :], wv_d[h]),
                       inc=d_wqkv, dma=True)

            qk_store_snap = [0, 0]
            v_store_snap = [0, 0]
            nrm_snap_par = [0, 0]
            nred_snap = [0]
            pq_free = [None, None, None, None]
            ln_done = [0] * NT
            sq_done = [0] * NT
            qkseq = [0]
            p.scope("lnqkv")

            def emit_ln_tile(tt):
                par = tt % 2
                wl = []
                if tt >= 2:
                    wl = [(c_ln, ln_done[tt - 2]), (c_sq, sq_done[tt - 2])]
                p.emit("sync",
                       lambda e, a=xb, pr=par, t=tt: e.dma_start(
                           a[:, pr, :], x_d[t * 128:(t + 1) * 128, :]),
                       waits=wl, inc=d_x, dma=True)
                hwait = [(d_h, 16 * (tt - 1))] if tt >= 2 else []
                emit_ln(xb[:, par, :], hb[:, par, :], sqb[:, par, :], stb[:, par, :],
                        c_sq, c_var, c_rstd, c_ln,
                        waits_src_act=[(d_x, d_x.n)],
                        waits_src_dve=[(d_x, d_x.n)],
                        extra_final_waits=hwait)
                sq_done[tt] = c_sq.n
                ln_done[tt] = c_ln.n
                p.emit("sync",
                       lambda e, a=hb, pr=par, t=tt: e.dma_start(
                           h_scr[t * 128:(t + 1) * 128, :], a[:, pr, :]),
                       waits=[(c_ln, c_ln.n)], inc=d_h, dma=True)
                for dt in range(8):
                    p.emit("scalar",
                           lambda e, d=dt, t=tt: e.dma_start_transpose(
                               hT[:, d, t * 128:(t + 1) * 128],
                               h_scr[t * 128:(t + 1) * 128, d * 128:(d + 1) * 128]),
                           waits=[(d_h, d_h.n)], inc=d_hT, dma=True)

            for tt in range(4):
                emit_ln_tile(tt)
            pending_nrm = []
            for ch in range(NCH):
                lnq = []
                if ch + 1 < NCH:
                    for tt in range(4 * ch + 4, 4 * ch + 8):
                        lnq.append(lambda t=tt: emit_ln_tile(t))

                p.scope(f"qk{ch}")
                for pr_i in range(NP):
                    for proj in range(2):
                        w_sb = (wq_sb, wk_sb)[proj]
                        par = qkseq[0] % 2
                        p4 = qkseq[0] % 4
                        qkseq[0] += 1

                        def qk_mms(e, w=w_sb, pi=pr_i, c=ch, pr=p4):
                            ins = None
                            for kt in range(8):
                                ins = e.matmul(pq[:, pr, :], w[:, pi, kt, :],
                                               hT[:, kt, c * 512:(c + 1) * 512],
                                               start=(kt == 0), stop=(kt == 7))
                            return ins
                        qw = [(d_wqkv, 16 * (2 * pr_i + 1 + proj)),
                              (d_hT, 16 * 32 * (ch + 1))]
                        if pq_free[p4] is not None:
                            qw.append(pq_free[p4])
                        p.emit("tensor", qk_mms, waits=qw, inc=c_qk)
                        if len(pending_nrm) >= 2:
                            pending_nrm.pop(0)()
                        if lnq and qkseq[0] % 3 == 2:
                            lnq.pop(0)()
                        p.emit("vector",
                               lambda e, pr=par, pp=p4: e.tensor_copy(
                                   ev_qk[:, pr, :], pq[:, pp, :]),
                               waits=[(c_qk, c_qk.n), (d_qk, qk_store_snap[par])],
                               inc=c_qkev)
                        pq_free[p4] = (c_qkev, c_qkev.n)
                        p.emit("vector",
                               lambda e, pr=par: e.tensor_tensor(
                                   sq2[:, pr, :], ev_qk[:, pr, :], ev_qk[:, pr, :], op=ALU.mult),
                               waits=[(c_nrm, nrm_snap_par[par])],
                               inc=c_sq2)

                        def nrm_mms(e, pr=par):
                            e.matmul(pnrm[0:1, 0, :], onesbf[0:64, 0:1], sq2[0:64, pr, :],
                                     start=True, stop=True, tile_position=(0, 0),
                                     skip_group_check=True)
                            return e.matmul(pnrm[0:1, 1, :], onesbf[64:128, 0:1],
                                            sq2[64:128, pr, :],
                                            start=True, stop=True, tile_position=(64, 0),
                                            skip_group_check=True)

                        def emit_nrm(par=par, sq2_cnt=c_sq2.n, nrm_fn=nrm_mms,
                                     idx2=((pr_i * 2 + proj) * 8 + ch) * 2):
                            p.emit("tensor", nrm_fn,
                                   waits=[(c_sq2, sq2_cnt), (c_nred, nred_snap[0]),
                                          (d_c, 32)],
                                   inc=c_nrm)
                            nrm_snap_par[par] = c_nrm.n
                            p.emit("vector",
                                   lambda e, i2=idx2: e.reduce_max(
                                       nmax[0:1, i2:i2 + 2], pnrm[0:1, 0:2, :],
                                       axis=AX.X),
                                   waits=[(c_nrm, c_nrm.n)], inc=c_nred)
                            nred_snap[0] = c_nred.n
                        pending_nrm.append(emit_nrm)
                        if proj == 0:
                            p.emit("sync",
                                   lambda e, pi=pr_i, c=ch, pr=par: e.dma_start(
                                       qT_scr[c, :, pi, :], ev_qk[:, pr, :]),
                                   waits=[(c_sq2, c_sq2.n)], inc=d_qk, dma=True)
                        else:
                            p.emit("sync",
                                   lambda e, pi=pr_i, c=ch, pr=par: e.dma_start(
                                       kT_scr[pi, :, c * 512:(c + 1) * 512], ev_qk[:, pr, :]),
                                   waits=[(c_sq2, c_sq2.n)], inc=d_qk, dma=True)
                        qk_store_snap[par] = d_qk.n

                p.scope(f"v{ch}")
                while pending_nrm:
                    pending_nrm.pop(0)()
                while lnq:
                    lnq.pop(0)()
                for tt in range(4 * ch, 4 * ch + 4):
                    for half in range(2):
                        par = qkseq[0] % 2
                        p4 = qkseq[0] % 4
                        qkseq[0] += 1

                        def v_mms(e, t=tt, h=half, pr=p4):
                            ins = None
                            for kt in range(8):
                                ins = e.matmul(pq[:, pr, :], hT[:, kt, t * 128:(t + 1) * 128],
                                               wv_sb[:, h, kt, :],
                                               start=(kt == 0), stop=(kt == 7))
                            return ins
                        vw = [(d_wqkv, 16 * (17 + half)), (d_hT, 16 * 32 * (ch + 1))]
                        if pq_free[p4] is not None:
                            vw.append(pq_free[p4])
                        p.emit("tensor", v_mms, waits=vw, inc=c_v)
                        p.emit("vector",
                               lambda e, pr=par, pp=p4: e.tensor_copy(
                                   ev_v[:, pr, :], pq[:, pp, :]),
                               waits=[(c_v, c_v.n), (d_v, v_store_snap[par])], inc=c_vev)
                        pq_free[p4] = (c_vev, c_vev.n)
                        p.emit("sync",
                               lambda e, t=tt, h=half, pr=par: e.dma_start(
                                   v_scr[h * 4:(h + 1) * 4, :, t, :, :].rearrange(
                                       "b p h d -> p b h d"),
                                   ev_v[:, pr, :].rearrange("p (b h d) -> p b h d",
                                                            b=4, h=2)),
                               waits=[(c_vev, c_vev.n)], inc=d_v, dma=True)
                        v_store_snap[par] = d_v.n

            p.scope("mbound")
            for pi in range(NP):
                for hl in range(2):
                    ms = pi * 8 + hl * 4
                    bq = pi * 32 + hl
                    bk = pi * 32 + 16 + hl
                    p.emit("vector",
                           lambda e, m=ms, b=bq: e.reduce_max(
                               mbh[0:1, m:m + 1], nmax[0:1, b:b + 15:2], axis=AX.X),
                           waits=[(c_nred, c_nred.n)])
                    p.emit("vector",
                           lambda e, m=ms, b=bk: e.reduce_max(
                               mbh[0:1, m + 1:m + 2], nmax[0:1, b:b + 15:2], axis=AX.X))
                    p.emit("vector",
                           lambda e, m=ms: e.tensor_tensor(
                               mbh[0:1, m + 2:m + 3], mbh[0:1, m:m + 1],
                               mbh[0:1, m + 1:m + 2], op=ALU.mult),
                           inc=c_mbv)
                    p.emit("scalar",
                           lambda e, m=ms: e.activation(
                               mbh[0:1, m + 2:m + 3], mbh[0:1, m + 2:m + 3], AF.Ln),
                           waits=[(c_mbv, c_mbv.n)])
                    p.emit("scalar",
                           lambda e, m=ms: e.activation(
                               mbh[0:1, m + 2:m + 3], mbh[0:1, m + 2:m + 3], AF.Exp, scale=0.5),
                           inc=c_rstd)
                    p.emit("vector",
                           lambda e, m=ms: e.tensor_scalar(
                               mbh[0:1, m + 3:m + 4], mbh[0:1, m + 2:m + 3],
                               -0.125, -1.0, op0=ALU.mult, op1=ALU.add),
                           waits=[(c_rstd, c_rstd.n)], inc=c_mbv)
                p.emit("vector",
                       lambda e, m=pi * 8: e.tensor_reduce(
                           mbh[0:1, m + 3:m + 4], mbh[0:1, m + 3:m + 8:4], axis=AX.X,
                           op=ALU.min),
                       inc=c_mbv)
                p.emit("tensor",
                       lambda e, m=pi * 8: e.matmul(pbc[:, 0:1], ones32[0:1, :],
                                                    mbh[0:1, m + 3:m + 4],
                                                    start=True, stop=True),
                       waits=[(c_mbv, c_mbv.n), (d_c, 48), (c_mb, c_mb.n)],
                       inc=c_mbb)
                p.emit("vector",
                       lambda e, g=pi: e.tensor_copy(biasg[:, g:g + 1], pbc[:, 0:1]),
                       waits=[(c_mbb, c_mbb.n)], inc=c_mb)

            qkv_reads_done = [(c_qk, c_qk.n), (c_v, c_v.n), (c_nrm, c_nrm.n),
                              (c_sq2, c_sq2.n), (c_qkev, c_qkev.n), (c_vev, c_vev.n),
                              (c_nred, c_nred.n), (c_mbv, c_mbv.n), (c_mb, c_mb.n),
                              (d_qk, d_qk.n), (d_v, d_v.n), (d_h, d_h.n),
                              (d_hT, d_hT.n), (d_x, d_x.n)]

        es2 = ExitStack()
        S2 = lambda n, sh, dt: es2.enter_context(nc.sbuf_tensor(n, sh, dt))
        P2 = lambda n, sh, dt: es2.enter_context(nc.psum_tensor(n, sh, dt))
        with es2:
            qTc = S2("qTc", [128, 2, 8, 512], BF16)
            kTp = S2("kTp", [128, 2, S], BF16)
            VSL = NT * 130 + 64
            vp = S2("vp", [128, 2, VSL], BF16)
            expT = S2("expT", [128, 2, 1024], BF16)
            oev = S2("oev", [128, 2, 512], F32)
            denb = S2("denb", [128, 8], F32)
            recb = S2("recb", [128, 8], F32)
            o_chunk = S2("o_chunk", [128, 2, 4, D], F32)
            xb2 = S2("xb2", [128, 2, D], F32)
            sqb2 = S2("sqb2", [128, 2, D], F32)
            stb2 = S2("stb2", [128, 4, 8], F32)
            lno = S2("lno", [128, 2, D], F32)
            xpb = S2("xpb", [128, 2, D], F32)
            h2b = S2("h2b", [128, 2, D], BF16)
            sc_ps = P2("sc_ps", [128, 2, 2, 512], F32)
            oP2 = P2("oP2", [128, 2, 512], F32)
            ttm = P2("ttm", [128, 8, 128], F32)

            bg = []

            def pump(n=1):
                for _ in range(n):
                    while bg:
                        try:
                            next(bg[0])
                            break
                        except StopIteration:
                            bg.pop(0)
                    else:
                        return

            def gen_ln(src, dst, sq, st, ch_sq, ch_var, ch_rstd, ch_out,
                       waits_src_act=(), waits_src_dve=(), extra_final_waits=()):
                p.emit("vector",
                       lambda e, s=src, q=sq: e.tensor_tensor(q, s, s, op=ALU.mult),
                       waits=list(waits_src_act) + list(waits_src_dve), inc=ch_sq)
                p.emit("vector",
                       lambda e, s=src, t=st: e.reduce_sum(t[:, 0:1], s, axis=AX.X),
                       waits=waits_src_dve)
                p.emit("vector",
                       lambda e, q=sq, t=st: e.reduce_sum(t[:, 1:2], q, axis=AX.X),
                       waits=[(ch_sq, ch_sq.n)])
                p.emit("vector",
                       lambda e, t=st: e.tensor_scalar(t[:, 2:3], t[:, 0:1], 1.0 / D, None, op0=ALU.mult))
                p.emit("vector",
                       lambda e, t=st: e.tensor_scalar(t[:, 3:4], t[:, 1:2], 1.0 / D, None, op0=ALU.mult))
                p.emit("vector",
                       lambda e, t=st: e.tensor_tensor(t[:, 4:5], t[:, 2:3], t[:, 2:3], op=ALU.mult))
                p.emit("vector",
                       lambda e, t=st: e.tensor_tensor(t[:, 5:6], t[:, 3:4], t[:, 4:5], op=ALU.subtract),
                       inc=ch_var)
                yield
                p.emit("scalar",
                       lambda e, t=st: e.activation(t[:, 6:7], t[:, 5:6], AF.Sqrt,
                                                    bias=epsb[0:t.shape[0], 0:1]),
                       waits=[(ch_var, ch_var.n), (c_cst, 1)], inc=ch_rstd)
                p.emit("vector",
                       lambda e, t=st: e.reciprocal(t[:, 7:8], t[:, 6:7]),
                       waits=[(ch_rstd, ch_rstd.n)], inc=ch_rstd)
                yield
                p.emit("vector",
                       lambda e, s=src, d=dst, t=st: e.tensor_scalar(
                           d, s, t[:, 2:3], t[:, 7:8], op0=ALU.subtract, op1=ALU.mult),
                       waits=[(ch_rstd, ch_rstd.n)] + list(extra_final_waits), inc=ch_out)

            res_done = [0] * NT
            rv_snap = [0, 0]
            rs_done_chunk = [0] * NCH

            def residual_gen(ch, div_all):
                for i in range(4):
                    tt = ch * 4 + i
                    par = tt % 2
                    wl = [(c_xp, res_done[tt - 2])] if tt >= 2 else []
                    p.emit("sync",
                           lambda e, t=tt, pr=par: e.dma_start(
                               xb2[:, pr, :], x_d[t * 128:(t + 1) * 128, :]),
                           waits=wl, inc=d_x2, dma=True)
                    yield
                    yield from gen_ln(o_chunk[:, ch % 2, i, :], lno[:, par, :],
                                      sqb2[:, par, :], stb2[:, par * 2, :],
                                      c_rs, c_rv, c_rr, c_lno,
                                      waits_src_act=[(c_div, div_all), (c_rv, rv_snap[par])],
                                      waits_src_dve=[(c_div, div_all)])
                    p.emit("vector",
                           lambda e, pr=par: e.tensor_tensor(
                               xpb[:, pr, :], xb2[:, pr, :], lno[:, pr, :], op=ALU.add),
                           waits=[(d_x2, d_x2.n), (d_xp, 16 * max(0, tt - 1))], inc=c_xp)
                    res_done[tt] = c_xp.n
                    p.emit("sync",
                           lambda e, t=tt, pr=par: e.dma_start(
                               xp_scr[t * 128:(t + 1) * 128, :], xpb[:, pr, :]),
                           waits=[(c_xp, c_xp.n)], inc=d_xp, dma=True)
                    yield
                    yield from gen_ln(xpb[:, par, :], h2b[:, par, :],
                                      sqb2[:, par, :], stb2[:, par * 2 + 1, :],
                                      c_rs, c_rv, c_rr, c_res,
                                      waits_src_act=[(c_xp, c_xp.n), (c_rv, rv_snap[par])],
                                      waits_src_dve=[(c_xp, c_xp.n)],
                                      extra_final_waits=[(d_h2, 16 * max(0, tt - 1))])
                    rv_snap[par] = c_rv.n
                    p.emit("sync",
                           lambda e, t=tt, pr=par: e.dma_start(
                               h2_scr[t * 128:(t + 1) * 128, :], h2b[:, pr, :]),
                           waits=[(c_res, c_res.n)], inc=d_h2, dma=True)
                    yield
                rs_done_chunk[ch] = c_rs.n

            for s2 in range(2):
                p.emit("vector",
                       lambda e, s=s2: e.memset(
                           vp[:, s, 0:NT * 130].rearrange(
                               "p (t h d) -> p t h d", t=NT, h=2)[:, :, :, 64:65],
                           1.0),
                       inc=c_cst)
                p.emit("vector",
                       lambda e, s=s2: e.memset(vp[:, s, NT * 130:VSL], 0.0),
                       inc=c_cst)
            for cc2 in range(2):
                p.emit("gpsimd",
                       lambda en, c2=cc2: en.dma_start(w1c[:, c2, :, :], wa1t_d[0, c2]),
                       inc=d_w13, dma=True)
                p.emit("gpsimd",
                       lambda en, c2=cc2: en.dma_start(w3c[:, c2, :, :], wa3t_d[0, c2]),
                       inc=d_w13, dma=True)

            kv_loaded = [0, 0]
            sc_snap_slot = [0, 0]
            av_snap_slot = [0, 0]
            qc_loaded = [0, 0]
            qc_sc_snap = [0, 0]
            exp_after_hist = [0, 0]
            av_hist = [0, 0]
            fin = {"oev": 0, "otr": 0, "div": 0}
            pending_fin = []

            def emit_kv_load(si, extra=()):
                ch2, pr2 = divmod(si, NP)
                sl = si % 2
                w = list(extra) + [(c_sc, sc_snap_slot[sl]),
                                   (d_qk, 16 * 2 * NP * NCH)]
                p.emit("sync",
                       lambda e, pi=pr2, s=sl: e.dma_start(kTp[:, s, :], kT_scr[pi]),
                       waits=w, inc=d_kv, dma=True)
                p.emit("sync",
                       lambda e, pi=pr2, s=sl: e.dma_start(
                           vp[:, s, 0:NT * 130].rearrange(
                               "p (t h d) -> p t h d", t=NT, h=2)[:, :, :, 0:64],
                           v_scr[pi]),
                       waits=[(d_v, 16 * NT * 2), (c_cst, 5),
                              (c_av, av_snap_slot[sl])],
                       inc=d_kv, dma=True)
                kv_loaded[sl] = d_kv.n

            emit_kv_load(0, extra=qkv_reads_done)

            for ch in range(NCH):
                cpar = ch % 2
                p.scope(f"att{ch}")
                if ch == 4:
                    for dt in range(8):
                        p.emit("sync",
                               lambda e, d=dt: e.dma_start_transpose(
                                   h2T_pre[:, d, :],
                                   h2_scr[0:QT, d * 128:(d + 1) * 128]),
                               waits=[(d_h2, 16 * 8)], inc=d_h2p, dma=True)
                qcw = [(d_qk, 16 * 2 * NP * NCH), (c_sc, qc_sc_snap[cpar])]
                if ch < 2:
                    qcw = qcw + list(qkv_reads_done)
                p.emit("gpsimd",
                       lambda e, c=ch, pr=cpar: e.dma_start(qTc[:, pr, :, :], qT_scr[c]),
                       waits=qcw, inc=d_qc, dma=True)
                qc_loaded[cpar] = d_qc.n
                for pr in range(NP):
                    si = ch * NP + pr
                    sl = si % 2
                    if si + 1 < NP * NCH:
                        emit_kv_load(si + 1,
                                     extra=qkv_reads_done if si == 0 else ())
                    my_kv = kv_loaded[sl]
                    av_base = c_av.n
                    exp_after = [0] * KT

                    def emit_scores(kt):
                        def sc_mms(e, k=kt, s=sl, pr2=pr, cp=cpar):
                            e.matmul(
                                sc_ps[:, k % 2, 0, :],
                                kTp[0:64, s, k * 128:(k + 1) * 128],
                                qTc[0:64, cp, pr2, :],
                                start=True, stop=True, tile_position=(0, 0),
                                skip_group_check=True)
                            return e.matmul(
                                sc_ps[:, k % 2, 1, :],
                                kTp[64:128, s, k * 128:(k + 1) * 128],
                                qTc[64:128, cp, pr2, :],
                                start=True, stop=True, tile_position=(64, 0),
                                skip_group_check=True)
                        scw = [(c_exp, exp_after[kt - 2] if kt >= 2 else exp_after_hist[kt % 2]),
                               (d_kv, my_kv), (d_qc, qc_loaded[cpar])]
                        if si == 0 and kt == 0:
                            scw = scw + list(qkv_reads_done)
                        p.emit("tensor", sc_mms, waits=scw, inc=c_sc)
                        p.emit("scalar",
                               lambda e, k=kt, pr2=pr: e.activation(
                                   expT[:, k % 2, :],
                                   sc_ps[:, k % 2, :, :].rearrange("p a b -> p (a b)"),
                                   AF.Exp, bias=biasg[:, pr2:pr2 + 1], scale=0.125),
                               waits=[(c_sc, c_sc.n), (c_mb, 8)],
                               inc=c_exp, nochain=True)
                        exp_after[kt] = c_exp.n

                    def emit_av(kt):
                        def av_mms(e, k=kt, s=sl):
                            off = k * 130
                            e.matmul(
                                oP2[:, 0, :], vp[:, s, off:off + 128],
                                expT[:, k % 2, 0:512],
                                start=(k == 0), stop=(k == KT - 1),
                                skip_group_check=True)
                            return e.matmul(
                                oP2[:, 1, :], vp[:, s, off + 65:off + 193],
                                expT[:, k % 2, 512:1024],
                                start=(k == 0), stop=(k == KT - 1),
                                skip_group_check=True)
                        wav = [(c_exp, exp_after[kt])]
                        if kt == 0:
                            wav.append((c_oev, fin["oev"]))
                        p.emit("tensor", av_mms, waits=wav, inc=c_av)
                        av_hist[kt % 2] = c_av.n

                    for kt in range(KT):
                        emit_scores(kt)
                        if kt == 0 and pending_fin:
                            pending_fin.pop(0)()
                        if kt >= 1:
                            emit_av(kt - 1)
                        pump(1)
                    emit_av(KT - 1)
                    exp_after_hist[0] = exp_after[KT - 2]
                    exp_after_hist[1] = exp_after[KT - 1]
                    sc_snap_slot[sl] = c_sc.n
                    av_snap_slot[sl] = c_av.n
                    qc_sc_snap[cpar] = c_sc.n

                    def make_finalize(si=si, pr=pr, cpar=cpar, ch=ch,
                                      av_base=av_base):
                        def fin_emit():
                            p.emit("vector",
                                   lambda e: e.tensor_copy(oev[0:65, :, :], oP2[0:65, :, :]),
                                   waits=[(c_av, av_base + KT), (c_otr, fin["otr"])],
                                   inc=c_oev)
                            fin["oev"] = c_oev.n

                            def otr_mms(e):
                                ins = None
                                for j in range(8):
                                    h, i = divmod(j, 4)
                                    ins = e.transpose(
                                        ttm[:, j, 0:65],
                                        oev[0:65, h, i * 128:(i + 1) * 128],
                                        id32[0:65, 0:65])
                                return ins
                            otw = [(c_oev, c_oev.n), (c_div, fin["div"]), (d_c, 16)]
                            if si == 0:
                                otw.extend(qkv_reads_done)
                            p.emit("tensor", otr_mms, waits=otw, inc=c_otr)
                            fin["otr"] = c_otr.n
                            p.emit("vector",
                                   lambda e: e.tensor_copy(
                                       denb[:], ttm[:, :, 64:65].rearrange("p a b -> p (a b)")),
                                   waits=[(c_otr, c_otr.n)])
                            p.emit("vector",
                                   lambda e: e.reciprocal(recb[:], denb[:]),
                                   inc=c_rcp)
                            dvw = [(c_rcp, c_rcp.n)]
                            if ch >= 2:
                                dvw.append((c_rs, rs_done_chunk[ch - 2]))
                            for j in range(8):
                                h, i = divmod(j, 4)
                                last = (j == 7)
                                p.emit("vector",
                                       lambda e, jj=j, ii=i, h2=h, pr2=pr, cp=cpar: e.tensor_scalar(
                                           o_chunk[:, cp, ii, pr2 * 128 + h2 * 64:pr2 * 128 + (h2 + 1) * 64],
                                           ttm[:, jj, 0:64],
                                           recb[:, jj:jj + 1], None,
                                           op0=ALU.mult),
                                       waits=dvw if j == 0 else [],
                                       inc=c_div if last else None)
                            fin["div"] = c_div.n
                        return fin_emit
                    pending_fin.append(make_finalize())
                    pump(2)
                while pending_fin:
                    pending_fin.pop(0)()
                bg.append(residual_gen(ch, c_div.n))
            while bg:
                pump(1)

            attn_done = [(c_sc, c_sc.n), (c_av, c_av.n),
                         (c_exp, c_exp.n), (c_div, c_div.n),
                         (c_rcp, c_rcp.n), (c_oev, c_oev.n),
                         (c_otr, c_otr.n), (c_lno, c_lno.n), (c_res, c_res.n),
                         (c_rs, c_rs.n), (c_rv, c_rv.n), (c_rr, c_rr.n),
                         (c_xp, c_xp.n),
                         (d_xp, d_xp.n), (d_h2, d_h2.n), (d_x2, d_x2.n),
                         (d_kv, d_kv.n), (d_qc, d_qc.n)]

        es3 = ExitStack()
        S3 = lambda n, sh, dt: es3.enter_context(nc.sbuf_tensor(n, sh, dt))
        P3 = lambda n, sh, dt: es3.enter_context(nc.psum_tensor(n, sh, dt))
        with es3:
            h2T = S3("h2T", [128, 8, QT], BF16)
            faq = S3("faq", [128, 8, QT], F32)
            zb = S3("zb", [128, 16, QT], BF16)
            zs = S3("zs", [128, QT], F32)
            w2d = S3("w2d", [128, 2, 16, 128], BF16)
            cwb = S3("cwb", [128, QT], F32)
            tmpf = S3("tmpf", [128, 512], F32)
            ctsb = S3("ctsb", [8, 128], F32)
            cwrow = S3("cwrow", [1, QT], F32)
            lgT = S3("lgT", [8, QT], F32)
            lgtm = S3("lgtm", [128, 8], F32)
            m1b = S3("m1b", [128, 4], F32)
            mskb = S3("mskb", [128, 8], F32)
            combtm = S3("combtm", [128, 8], F32)
            ftile = S3("ftile", [128, 2, D], F32)
            sqb3 = S3("sqb3", [128, 2, D], F32)
            stb3 = S3("stb3", [128, 2, 8], F32)
            lnf = S3("lnf", [128, 2, D], F32)
            xpl = S3("xpl", [128, 2, D], F32)
            outb = S3("outb", [128, 2, D], F32)
            z1_ps = P3("z1_ps", [128, 2, 512], F32)
            z3_ps = P3("z3_ps", [128, 2, 512], F32)
            y2_ps = P3("y2_ps", [128, 2, 512], F32)
            mi_ps = P3("mi_ps", [128, 2, 512], F32)



            psum_done = [(c_exp, c_exp.n), (c_oev, c_oev.n), (c_div, c_div.n)]

            w13_done = {}
            w2_done = {}
            cast13_piece = {}
            cast2_piece = {}
            slot13_store = [0, 0]
            slot2_store = [0, 0]
            z1_snap = [0, 0]
            z3_snap = [0, 0]
            y2_snap = [0, 0]
            z_reads = 0
            h2T_reads = 0
            fu_all = 0
            fsnap = {"ftr": 0, "ftc": 0}
            fv_snap = [0, 0]
            cw_read_snap = 0

            def emit_w13_load(q, e_i, c, extra=()):
                gi = q * NE + e_i
                need = gi * 16 + 4 * c - 4
                pr = c % 2
                w = list(extra) + ([(c_z3, need)] if need > 0 else []) \
                    + [(d_cbk, slot13_store[pr])]
                if q == 0:
                    p.emit("gpsimd",
                           lambda en, ei=e_i, cc=c, pr=pr: en.dma_start(
                               w1c[:, pr, :, :], wa1t_d[ei, cc]),
                           waits=w, inc=d_w13, dma=True)
                    n1 = d_w13.n
                    p.emit("gpsimd",
                           lambda en, ei=e_i, cc=c, pr=pr: en.dma_start(
                               w3c[:, pr, :, :], wa3t_d[ei, cc]),
                           inc=d_w13, dma=True)
                    n3 = d_w13.n
                    p.emit("sync",
                           lambda en, ei=e_i, cc=c, pr=pr: en.dma_start(
                               wa1b_d[ei, cc], w1c[:, pr, :, :]),
                           waits=[(d_w13, n1)], inc=d_cbk, dma=True)
                    p.emit("sync",
                           lambda en, ei=e_i, cc=c, pr=pr: en.dma_start(
                               wa3b_d[ei, cc], w3c[:, pr, :, :]),
                           waits=[(d_w13, n3)], inc=d_cbk, dma=True)
                    cast13_piece[(e_i, c)] = d_cbk.n
                    slot13_store[pr] = d_cbk.n
                else:
                    w.append((d_cbk, cast13_piece[(e_i, c)]))
                    p.emit("gpsimd",
                           lambda en, ei=e_i, cc=c, pr=pr: en.dma_start(
                               w1c[:, pr, :, :], wa1b_d[ei, cc]),
                           waits=w, inc=d_w13, dma=True)
                    p.emit("gpsimd",
                           lambda en, ei=e_i, cc=c, pr=pr: en.dma_start(
                               w3c[:, pr, :, :], wa3b_d[ei, cc]),
                           inc=d_w13, dma=True)
                w13_done[(q, e_i, c)] = d_w13.n

            def emit_w2_load(q, e_i, dt, extra=()):
                gi = q * NE + e_i
                need = gi * 16 + 2 * dt - 2
                w = list(extra) + ([(c_y2, need)] if need > 0 else [])
                if gi == 0 and dt < 2:
                    w = w + list(attn_done)
                w.append((d_cbk, slot2_store[dt % 2]))
                if q == 0:
                    p.emit("gpsimd",
                           lambda en, ei=e_i, d2=dt: en.dma_start(
                               w2d[:, d2 % 2, :, :], wa2t_d[ei, d2]),
                           waits=w, inc=d_w2, dma=True)
                    n2 = d_w2.n
                    p.emit("sync",
                           lambda en, ei=e_i, d2=dt: en.dma_start(
                               wa2b_d[ei, d2], w2d[:, d2 % 2, :, :]),
                           waits=[(d_w2, n2)], inc=d_cbk, dma=True)
                    cast2_piece[(e_i, dt)] = d_cbk.n
                    slot2_store[dt % 2] = d_cbk.n
                else:
                    w.append((d_cbk, cast2_piece[(e_i, dt)]))
                    p.emit("gpsimd",
                           lambda en, ei=e_i, d2=dt: en.dma_start(
                               w2d[:, d2 % 2, :, :], wa2b_d[ei, d2]),
                           waits=w, inc=d_w2, dma=True)
                w2_done[(q, e_i, dt)] = d_w2.n

            wload_seq = []
            for q in range(NQ):
                for e_i in range(NE):
                    for c in range(4):
                        wload_seq.append(("w13", q, e_i, c))
                    for dt in range(8):
                        wload_seq.append(("w2", q, e_i, dt))
            wl_idx = [0]

            def pump_wloads(n, extra=()):
                for _ in range(n):
                    if wl_idx[0] >= len(wload_seq):
                        return
                    item = wload_seq[wl_idx[0]]
                    wl_idx[0] += 1
                    if item[0] == "w2":
                        emit_w2_load(item[1], item[2], item[3], extra=extra)
                    else:
                        emit_w13_load(item[1], item[2], item[3], extra=extra)

            h2T_loaded = [None] * NQ
            h2T_read_snap = [0, 0, 0, 0]

            def h2T_tile(qq):
                return h2T_pre if qq % 2 == 0 else h2T

            h2T_loaded[0] = (d_h2p, d_h2p.n)

            def emit_h2T_load(qq):
                w = [(d_h2, 16 * 4 * (2 * qq + 2))]
                if qq == 1:
                    w += list(attn_done)
                elif qq >= 2:
                    w += [(c_z1, h2T_read_snap[qq - 2]), (c_lg, 2 * (qq - 1))]
                eng = "sync" if qq % 2 == 0 else "scalar"
                ch2 = d_h2p if qq % 2 == 0 else d_h2T
                dst = h2T_tile(qq)
                for dt in range(8):
                    p.emit(eng,
                           lambda e, d=dt, q2=qq, t=dst: e.dma_start_transpose(
                               t[:, d, :],
                               h2_scr[q2 * QT:(q2 + 1) * QT, d * 128:(d + 1) * 128]),
                           waits=w, inc=ch2, dma=True)
                h2T_loaded[qq] = (ch2, ch2.n)

            wl_idx[0] = 2
            w13_done[(0, 0, 0)] = 32
            w13_done[(0, 0, 1)] = 64
            for cc2 in range(2):
                p.emit("sync",
                       lambda en, c2=cc2: en.dma_start(wa1b_d[0, c2], w1c[:, c2, :, :]),
                       waits=[(d_w13, 16 * (2 * cc2 + 1))], inc=d_cbk, dma=True)
                p.emit("sync",
                       lambda en, c2=cc2: en.dma_start(wa3b_d[0, c2], w3c[:, c2, :, :]),
                       waits=[(d_w13, 16 * (2 * cc2 + 2))], inc=d_cbk, dma=True)
                cast13_piece[(0, cc2)] = d_cbk.n
                slot13_store[cc2] = d_cbk.n
            pump_wloads(1)
            emit_h2T_load(1)
            pending_ffn = []
            for q in range(NQ):
                p.scope(f"ffq{q}")

                def lg_mms(e, half, qq):
                    ins = None
                    t = h2T_tile(qq)
                    for dt in range(8):
                        ins = e.matmul(mi_ps[0:8, half, :], rwt_sb[:, dt, :],
                                       t[:, dt, half * 512:(half + 1) * 512],
                                       start=(dt == 0), stop=(dt == 7))
                    return ins
                for half in range(2):
                    lgw = [(d_rw, 16), h2T_loaded[q],
                           (c_lgev, c_lgev.n), (c_ftc, fsnap["ftc"]), (c_cw, c_cw.n)]
                    if q == 0 and half == 0:
                        lgw += [(c_rcp, c_rcp.n), (c_div, c_div.n),
                                (c_av, c_av.n), (c_exp, c_exp.n),
                                (c_oev, c_oev.n), (c_otr, c_otr.n)]
                    p.emit("tensor",
                           lambda e, h=half, qq=q: lg_mms(e, h, qq),
                           waits=lgw, inc=c_lg)
                p.emit("vector",
                       lambda e: e.tensor_scalar(
                           lgT[:], mi_ps[0:8, :, :].rearrange("p a b -> p (a b)"),
                           ebias[:, 0:1], None, op0=ALU.add),
                       waits=[(c_lg, c_lg.n), (d_c, 64), (c_ltr, c_ltr.n)],
                       inc=c_lgev)
                lgev_now = c_lgev.n

                def make_tk(i, q, lgev_now=None):
                    def tk_emit():
                        p.emit("tensor",
                               lambda e, ii=i: e.transpose(
                                   mi_ps[:, 0, 0:8], lgT[0:8, ii * 128:(ii + 1) * 128],
                                   id32[0:8, 0:8]),
                               waits=[(c_lgev, lgev_now), (c_tk, c_tk.n),
                                      (c_ftc, c_ftc.n)],
                               inc=c_ltr)
                        p.emit("vector",
                               lambda e: e.tensor_copy(lgtm[:], mi_ps[:, 0, 0:8]),
                               waits=[(c_ltr, c_ltr.n)], inc=c_tk)
                        p.emit("vector", lambda e: e.reduce_max(m1b[:, 0:1], lgtm[:], axis=AX.X))
                        p.emit("vector",
                               lambda e: e.tensor_scalar(mskb[:], lgtm[:], m1b[:, 0:1], -1e30,
                                                         op0=ALU.is_ge, op1=ALU.mult))
                        p.emit("vector",
                               lambda e: e.tensor_tensor(mskb[:], lgtm[:], mskb[:], op=ALU.add))
                        p.emit("vector", lambda e: e.reduce_max(m1b[:, 1:2], mskb[:], axis=AX.X))
                        p.emit("vector",
                               lambda e: e.tensor_scalar(m1b[:, 2:3], m1b[:, 1:2], m1b[:, 0:1], None,
                                                         op0=ALU.subtract),
                               inc=c_tk)
                        p.emit("scalar",
                               lambda e: e.activation(m1b[:, 2:3], m1b[:, 2:3], AF.Exp),
                               waits=[(c_tk, c_tk.n)], inc=c_rr)
                        p.emit("vector",
                               lambda e: e.tensor_scalar(m1b[:, 2:3], m1b[:, 2:3], 1.0, None, op0=ALU.add),
                               waits=[(c_rr, c_rr.n)])
                        p.emit("vector", lambda e: e.reciprocal(m1b[:, 3:4], m1b[:, 2:3]))
                        p.emit("vector",
                               lambda e: e.tensor_scalar(combtm[:], lgtm[:], m1b[:, 0:1], None,
                                                         op0=ALU.subtract),
                               inc=c_tk)
                        p.emit("scalar",
                               lambda e: e.activation(combtm[:], combtm[:], AF.Exp),
                               waits=[(c_tk, c_tk.n)], inc=c_rr)
                        p.emit("vector",
                               lambda e: e.tensor_scalar(mskb[:], lgtm[:], m1b[:, 1:2], None,
                                                         op0=ALU.is_ge),
                               waits=[(c_rr, c_rr.n)])
                        p.emit("vector",
                               lambda e: e.tensor_tensor(combtm[:], combtm[:], mskb[:], op=ALU.mult))
                        p.emit("vector",
                               lambda e: e.tensor_scalar(combtm[:], combtm[:], m1b[:, 3:4], None,
                                                         op0=ALU.mult),
                               inc=c_tk)
                        p.emit("tensor",
                               lambda e: e.transpose(mi_ps[0:8, 1, 0:128], combtm[:], id32[:]),
                               waits=[(c_tk, c_tk.n), (c_cmb, c_cmb.n)], inc=c_ctr)
                        p.emit("vector",
                               lambda e: e.tensor_copy(ctsb[:], mi_ps[0:8, 1, 0:128]),
                               waits=[(c_ctr, c_ctr.n), (d_cws, d_cws.n)], inc=c_cmb)
                        p.emit("scalar",
                               lambda e, t=q * 8 + i: e.dma_start(
                                   cwT_scr[:, t * 128:(t + 1) * 128], ctsb[:]),
                               waits=[(c_cmb, c_cmb.n)], inc=d_cws, dma=True)
                    return tk_emit
                for i in range(8):
                    pending_ffn.append(make_tk(i, q, lgev_now))

                for e_i in range(NE):
                    routed = e_i >= 2
                    if routed:
                        while pending_ffn:
                            pending_ffn.pop(0)()
                        p.emit("scalar",
                               lambda e, ei=e_i, qq=q: e.dma_start(
                                   cwrow[:], cwT_scr[ei - 2:ei - 1, qq * QT:(qq + 1) * QT]),
                               waits=[(d_cws, 16 * 8 * (q + 1)), (c_cwm, c_cwm.n)],
                               inc=d_cwl, dma=True)
                        for half in range(2):
                            p.emit("tensor",
                                   lambda e, h=half: e.matmul(
                                       mi_ps[:, h, :], ones32[0:1, :],
                                       cwrow[0:1, h * 512:(h + 1) * 512],
                                       start=True, stop=True),
                                   waits=[(d_cwl, d_cwl.n), (c_ftc, fsnap["ftc"]),
                                          (c_ltr, c_ltr.n), (c_ctr, c_ctr.n),
                                          (c_cw, c_cw.n)],
                                   inc=c_cwm)
                        p.emit("vector",
                               lambda e: e.tensor_copy(
                                   cwb[:], mi_ps[:, :, :].rearrange("p a b -> p (a b)")),
                               waits=[(c_cwm, c_cwm.n), (c_fu, cw_read_snap)],
                               inc=c_cw)

                    for ht in range(16):
                        c = ht // 4
                        pump_wloads(1)
                        if pending_ffn and (e_i > 0 or ht >= 4):
                            pending_ffn.pop(0)()

                        def z_mms(e, w, ps, h=ht, cc=c, qq=q):
                            ins = None
                            t = h2T_tile(qq)
                            for half in range(2):
                                for dt in range(8):
                                    ins = e.matmul(
                                        ps[:, half, :],
                                        w[:, cc % 2, dt, (h % 4) * 128:(h % 4 + 1) * 128],
                                        t[:, dt, half * 512:(half + 1) * 512],
                                        start=(dt == 0), stop=(dt == 7),
                                        skip_group_check=True)
                            return ins
                        z1w = [(d_w13, w13_done[(q, e_i, c)]),
                               h2T_loaded[q],
                               (c_si, z1_snap[0])]
                        if q == 0 and e_i == 0 and ht == 0:
                            z1w += psum_done
                        p.emit("tensor",
                               lambda e, h=ht, cc=c, qq=q: z_mms(e, w1c, z1_ps, h, cc, qq),
                               waits=z1w,
                               inc=c_z1)
                        z1_snap[0] = c_si.n + 1
                        h2T_reads = c_z1.n
                        z3w = [(c_zm, z3_snap[0])]
                        if q == 0 and e_i == 0 and ht == 0:
                            z3w += psum_done
                        p.emit("tensor",
                               lambda e, h=ht, cc=c, qq=q: z_mms(e, w3c, z3_ps, h, cc, qq),
                               waits=z3w,
                               inc=c_z3)
                        z3_snap[0] = c_zm.n + 1
                        p.emit("scalar",
                               lambda e: e.activation(
                                   zs[:], z1_ps[:, :, :].rearrange("p a b -> p (a b)"), AF.Silu),
                               waits=[(c_z1, c_z1.n), (c_zm, c_zm.n)], inc=c_si)
                        p.emit("vector",
                               lambda e, h=ht: e.tensor_tensor(
                                   zb[:, h, :], zs[:],
                                   z3_ps[:, :, :].rearrange("p a b -> p (a b)"), op=ALU.mult),
                               waits=[(c_si, c_si.n), (c_z3, c_z3.n), (c_y2, z_reads)],
                               inc=c_zm)
                    zm_all = c_zm.n

                    for dt in range(8):
                        for half in range(2):
                            par = (dt * 2 + half) % 2

                            def y2_mms(e, d=dt, h=half, pr=par):
                                ins = None
                                for ht in range(16):
                                    ins = e.matmul(
                                        y2_ps[:, pr, :],
                                        w2d[:, d % 2, ht, :],
                                        zb[:, ht, h * 512:(h + 1) * 512],
                                        start=(ht == 0), stop=(ht == 15),
                                        skip_group_check=True)
                                return ins
                            y2w = [(c_zm, zm_all), (d_w2, w2_done[(q, e_i, dt)]),
                                   (c_fu, y2_snap[par])]
                            if q == 0 and e_i == 0 and dt == 0:
                                y2w += psum_done
                            p.emit("tensor", y2_mms, waits=y2w, inc=c_y2)
                            if e_i == 0:
                                p.emit("vector",
                                       lambda e, d=dt, h=half, pr=par: e.tensor_copy(
                                           faq[:, d, h * 512:(h + 1) * 512], y2_ps[:, pr, :]),
                                       waits=[(c_y2, c_y2.n), (c_ftr, fsnap["ftr"])],
                                       inc=c_fu)
                            elif not routed:
                                p.emit("vector",
                                       lambda e, d=dt, h=half, pr=par: e.scalar_tensor_tensor(
                                           faq[:, d, h * 512:(h + 1) * 512], y2_ps[:, pr, :],
                                           1.0, faq[:, d, h * 512:(h + 1) * 512],
                                           op0=ALU.mult, op1=ALU.add),
                                       waits=[(c_y2, c_y2.n)], inc=c_fu)
                            else:
                                p.emit("vector",
                                       lambda e, h=half, pr=par: e.tensor_tensor(
                                           tmpf[:], y2_ps[:, pr, :],
                                           cwb[:, h * 512:(h + 1) * 512], op=ALU.mult),
                                       waits=[(c_y2, c_y2.n), (c_cw, c_cw.n)])
                                p.emit("vector",
                                       lambda e, d=dt, h=half: e.tensor_tensor(
                                           faq[:, d, h * 512:(h + 1) * 512],
                                           faq[:, d, h * 512:(h + 1) * 512], tmpf[:], op=ALU.add),
                                       inc=c_fu)
                            y2_snap[par] = c_fu.n
                    z_reads = c_y2.n
                    w2_reads_snap = c_y2.n
                    cw_read_snap = c_fu.n
                fu_all = c_fu.n
                h2T_read_snap[q] = c_z1.n
                if q + 2 < NQ:
                    emit_h2T_load(q + 2)

                p.scope(f"fin{q}")
                oadd_done = [c_oadd.n, c_oadd.n]

                def make_qfin(i, q=q, fu_all_q=None, oadd_done=oadd_done):
                    tt = q * 8 + i
                    par = i % 2

                    def qfin_emit():
                        def mi8(dt):
                            return mi_ps[:, dt // 4, (dt % 4) * 128:(dt % 4 + 1) * 128]

                        def ftr_mms(e, ii=i, mi8=mi8):
                            ins = None
                            for dt in range(8):
                                ins = e.transpose(mi8(dt),
                                                  faq[:, dt, ii * 128:(ii + 1) * 128], id32[:])
                            return ins
                        p.emit("tensor", ftr_mms,
                               waits=[(c_fu, fu_all_q), (c_ftc, c_ftc.n),
                                      (c_cw, c_cw.n), (c_ltr, c_ltr.n),
                                      (c_lgev, c_lgev.n), (c_ctr, c_ctr.n),
                                      (c_cwm, c_cwm.n)],
                               inc=c_ftr)
                        fsnap["ftr"] = c_ftr.n
                        p.emit("vector",
                               lambda e, pr=par: e.tensor_copy(
                                   ftile[:, pr, :], mi_ps[:, :, :].rearrange("p a b -> p (a b)")),
                               waits=[(c_ftr, c_ftr.n)], inc=c_ftc)
                        fsnap["ftc"] = c_ftc.n
                        emit_ln(ftile[:, par, :], lnf[:, par, :], sqb3[:, par, :], stb3[:, par, :],
                                c_fs, c_fv, c_fr, c_fev,
                                waits_src_act=[(c_ftc, c_ftc.n), (c_fv, fv_snap[par])],
                                waits_src_dve=[(c_ftc, c_ftc.n)])
                        fv_snap[par] = c_fv.n
                        p.emit("sync",
                               lambda e, t=tt, pr=par: e.dma_start(
                                   xpl[:, pr, :], xp_scr[t * 128:(t + 1) * 128, :]),
                               waits=[(d_xp, 16 * (tt + 1)), (c_oadd, oadd_done[par])],
                               inc=d_xpl, dma=True)
                        p.emit("vector",
                               lambda e, pr=par: e.tensor_tensor(
                                   outb[:, pr, :], xpl[:, pr, :], lnf[:, pr, :], op=ALU.add),
                               waits=[(d_xpl, d_xpl.n), (d_out, 16 * max(0, tt - 1))],
                               inc=c_oadd)
                        oadd_done[par] = c_oadd.n
                        p.emit("sync",
                               lambda e, t=tt, pr=par: e.dma_start(
                                   out_d[t * 128:(t + 1) * 128, :], outb[:, pr, :]),
                               waits=[(c_oadd, c_oadd.n)], inc=d_out, dma=True)
                    return qfin_emit

                for i in range(8):
                    pending_ffn.append(make_qfin(i, fu_all_q=fu_all))

            while pending_ffn:
                pending_ffn.pop(0)()
            p.emit("sync", lambda e: e.nop(), waits=[(d_out, 16 * NT)])

        p.simulate()

        block = es0.enter_context(nc.Block())
        p.build2(block)

    return nc


_CACHE = {}


def _get_nc():
    if "nc" not in _CACHE:
        _CACHE["nc"] = build()
    return _CACHE["nc"]


def kernel(L0_attn_inputs, L0_attn_Wq, L0_attn_Wk, L0_attn_Wv,
           L0_ffn_router_w, L0_ffn_expert_bias, L0_ffn_w1, L0_ffn_w3,
           L0_ffn_w2, L0_ffn_sw1, L0_ffn_sw3, L0_ffn_sw2):
    nc = _get_nc()
    f32 = np.float32
    ac = np.ascontiguousarray
    x = np.asarray(L0_attn_inputs, dtype=f32)
    def _wqk(w):
        return ac(np.asarray(w, f32).reshape(8, 128, 8, 128).transpose(2, 1, 0, 3))
    wq, wk = _wqk(L0_attn_Wq), _wqk(L0_attn_Wk)
    wv = ac(np.asarray(L0_attn_Wv, f32).reshape(8, 128, 2, 512).transpose(2, 1, 0, 3))
    rwt = ac(np.asarray(L0_ffn_router_w, f32).T.reshape(8, 128, 8).transpose(1, 0, 2))
    eb = ac(np.asarray(L0_ffn_expert_bias, f32).reshape(8, 1))
    def _w13(sw, w):
        a = np.concatenate([np.asarray(sw, f32), np.asarray(w, f32)], 0).transpose(0, 2, 1)
        return ac(a.reshape(NE, 8, 128, 4, 512).transpose(0, 3, 2, 1, 4))
    wa1t = _w13(L0_ffn_sw1, L0_ffn_w1)
    wa3t = _w13(L0_ffn_sw3, L0_ffn_w3)
    a2 = np.concatenate([np.asarray(L0_ffn_sw2, f32), np.asarray(L0_ffn_w2, f32)], 0) \
           .transpose(0, 2, 1)
    wa2t = ac(a2.reshape(NE, 16, 128, 8, 128).transpose(0, 3, 2, 1, 4))
    id32 = np.eye(128, dtype=f32)
    onesbf = np.ones((128, 1), dtype=ml_dtypes.bfloat16)
    ones32 = np.ones((1, 128), dtype=f32)

    in_maps = []
    for c in range(8):
        in_maps.append({
            "x": ac(x[c]), "wq": wq, "wk": wk, "wv": wv,
            "rwt": rwt, "eb": eb, "wa1t": wa1t, "wa3t": wa3t, "wa2t": wa2t,
            "id32": id32, "onesbf": onesbf, "ones32": ones32,
        })
    res = run_bass_kernel_spmd(nc, in_maps, core_ids=list(range(8)))
    return np.stack([r["out"] for r in res.results], axis=0)
